# version 04
# baseline (speedup 1.0000x reference)
import numpy as np
import ml_dtypes
import concourse.bass as bass
import concourse.mybir as mybir
from concourse.bass_utils import run_bass_kernel_spmd
from concourse.bass_isa import ReduceOp

F32 = mybir.dt.float32
BF16 = mybir.dt.bfloat16
AF = mybir.ActivationFunctionType
ALU = mybir.AluOpType

D = 1024
NF = 8
TC = 512
NTT = TC // 128

ENGS = ("sync", "gpsimd", "vector", "scalar", "tensor")


class Prog:

    def __init__(self):
        self.ops = {e: [] for e in ENGS}
        self.sems = {}
        self.prod_at = {e: {} for e in ENGS}
        self.waited = {e: {} for e in ENGS}

    def sem(self, name, producer, step=1):
        self.sems[name] = dict(p=producer, c=0, step=step)

    def mark(self, name):
        return self.sems[name]["c"]

    def add(self, eng, fn, waits=(), inc=None, inc_n=None, fwaits=()):
        w = []
        for (s, v) in waits:
            if v is None or v <= 0:
                continue
            sem = self.sems[s]
            if sem["p"] == eng and self.prod_at[eng].get(s, 0) >= v:
                continue
            if self.waited[eng].get(s, 0) >= v:
                continue
            self.waited[eng][s] = v
            w.append((s, v))
        for (s, v) in fwaits:
            if v is None or v <= 0:
                continue
            if self.waited[eng].get(s, 0) >= v:
                continue
            self.waited[eng][s] = v
            w.append((s, v))
        incs = []
        if inc is not None:
            n = inc_n if inc_n is not None else self.sems[inc]["step"]
            self.sems[inc]["c"] += n
            incs.append((inc, n))
            if self.sems[inc]["p"] == eng:
                self.prod_at[eng][inc] = self.sems[inc]["c"]
        self.ops[eng].append((w, fn, incs))
        return self.mark(inc) if inc else None

    def emit(self, nc):
        import contextlib
        with contextlib.ExitStack() as st:
            handles = {}
            for name in self.sems:
                handles[name] = st.enter_context(nc.semaphore(name))
            for name in getattr(self, "extra_sems", []):
                handles[name] = st.enter_context(nc.semaphore(name))
            self.handles = handles
            block = st.enter_context(nc.Block())

            def run(eng_name, eng):
                for (w, fn, incs) in self.ops[eng_name]:
                    for (s, v) in w:
                        eng.wait_ge(handles[s], v)
                    inst = fn(eng)
                    for (s, n) in incs:
                        inst.then_inc(handles[s], n)

            @block.sync
            def _(sync):
                run("sync", sync)

            @block.gpsimd
            def _(gpsimd):
                run("gpsimd", gpsimd)

            @block.vector
            def _(vector):
                run("vector", vector)

            @block.scalar
            def _(scalar):
                run("scalar", scalar)

            @block.tensor
            def _(tensor):
                run("tensor", tensor)


class Banks:

    def __init__(self, prog):
        self.p = prog
        self.state = {}

    def reg(self, name, tensor):
        self.state[name] = dict(t=tensor, readers=[], writer=None)

    def t(self, name):
        return self.state[name]["t"]

    def write_waits(self, name):
        st = self.state[name]
        w = list(st["readers"])
        st["readers"] = []
        return w

    def wrote(self, name, sem, val):
        self.state[name]["writer"] = (sem, val)

    def read_waits(self, name):
        st = self.state[name]
        return [st["writer"]] if st["writer"] else []

    def read(self, name, sem, val):
        self.state[name]["readers"].append((sem, val))


def build(n_chunks=16, debug=False):
    nc = bass.Bass("TRN2", target_bir_lowering=False)
    TOK = n_chunks * TC

    x_d = nc.dram_tensor("x", [TOK + 3, D], F32, kind="ExternalInput")
    wrT_d = nc.dram_tensor("wrT", [D, D], F32, kind="ExternalInput")
    wiT_d = nc.dram_tensor("wiT", [D, D], F32, kind="ExternalInput")
    woT_d = nc.dram_tensor("woT", [D, D], F32, kind="ExternalInput")
    wfT_d = nc.dram_tensor("wfT", [D, D], F32, kind="ExternalInput")
    convd_d = nc.dram_tensor("convd", [32, 128, 128], F32, kind="ExternalInput")
    br_d = nc.dram_tensor("br_fm", [128, NF], F32, kind="ExternalInput")
    bi_d = nc.dram_tensor("bi_fm", [128, NF], F32, kind="ExternalInput")
    cb_d = nc.dram_tensor("cb_fm", [128, NF], F32, kind="ExternalInput")
    la_d = nc.dram_tensor("la_fm", [128, NF], F32, kind="ExternalInput")
    gam_d = nc.dram_tensor("gam_fm", [128, NF], F32, kind="ExternalInput")
    consts_d = nc.dram_tensor("consts", [128, 8], F32, kind="ExternalInput")
    eye_d = nc.dram_tensor("eye", [128, 128], F32, kind="ExternalInput")
    eyebf_d = nc.dram_tensor("eye_bf", [128, 128], BF16, kind="ExternalInput")
    ones_d = nc.dram_tensor("ones_bf", [128, 1], BF16, kind="ExternalInput")
    rmask_d = nc.dram_tensor("rmask", [128, NF], F32, kind="ExternalInput")
    out_d = nc.dram_tensor("out", [TOK, D], F32, kind="ExternalOutput")
    h_scr = nc.dram_tensor("h_scr", [n_chunks, 128, NF * TC], BF16)
    a_scr = nc.dram_tensor("a_scr", [n_chunks, 128, NF * TC], BF16)
    g_scr = nc.dram_tensor("g_scr", [n_chunks, 128, NF * TC], BF16)
    sh_x = nc.dram_tensor("sh_x", [2, 128, NF], F32, addr_space="Shared")
    sh_f = nc.dram_tensor("sh_f", [2, 16], mybir.dt.uint32, addr_space="Shared")
    slot_d = nc.dram_tensor("slot", [1, 4], mybir.dt.uint32, kind="ExternalInput")
    nonce_d = nc.dram_tensor("nonce", [1, 1], mybir.dt.uint32, kind="ExternalInput")

    A = nc.alloc_sbuf_tensor
    wr_bf = A("wr_bf", [128, NF * D], BF16)
    wi_bf = A("wi_bf", [128, NF * D], BF16)
    wo_bf = A("wo_bf", [128, NF * D], BF16)
    wf_bf = A("wf_bf", [128, NF * D], BF16)
    conv_bf = A("conv_bf", [128, 32 * 128], BF16)

    x_ln = A("x_ln", [128, 2, D], F32)
    xn_tok = A("xn_tok", [128, NTT * D], BF16)
    xn_feat = A("xn_feat", [128, NF, 515], BF16)
    xcb = A("xcb", [128, NF * TC], BF16)
    b_t = A("b_t", [128, NF * TC], F32)
    a_t = A("a_t", [128, NF * TC], F32)
    h_t = A("h_t", [128, NF * TC], F32)
    sct = A("sct", [128, NF * TC], BF16)
    A_t = A("A_t", [128, NF * TC], F32)
    g_t = A("g_t", [128, NF * TC], BF16)
    zeros_t = A("zeros_t", [128, TC], F32)
    gat = A("gat", [128, NF], F32)
    rmask = A("rmask_s", [128, NF], F32)
    slot_s = A("slot_s", [1, 4], mybir.dt.uint32)
    nonce_s = A("nonce_s", [1, 1], mybir.dt.uint32)
    flag_s = A("flag_s", [1, 1], mybir.dt.uint32)
    hent = A("hent", [128, NF], F32)
    acar = A("acar", [128, NF], F32)
    sq_rot = A("sq_rot", [128, 2, 256], BF16)
    rt_rot = A("rt_rot", [128, 512], F32)
    it_rot = A("it_rot", [128, 2, 512], BF16)
    tt_rot = A("tt_rot", [128, 2, 512], BF16)
    ff_rot = A("ff_rot", [128, 2, 512], BF16)
    carry = A("carry", [128, NF], F32)
    spc = A("spc", [128, 1], F32)

    st6 = A("st6", [128, 2, 12], F32)
    mv = A("mv", [128, 4, 2], F32)
    sd_t = A("sd_t", [128, 4], F32)
    rs_t = A("rs_t", [128, 4], F32)
    nb_t = A("nb_t", [128, 4], F32)
    rms_row = A("rms_row", [1, 256], F32)
    rms_tok = A("rms_tok", [128, 4], F32)

    br2 = A("br2", [128, NF], F32)
    bi2 = A("bi2", [128, NF], F32)
    cb_fm = A("cb_fm_s", [128, NF], F32)
    c4 = A("c4", [128, NF], F32)
    hgam = A("hgam", [128, NF], F32)
    la_fm = A("la_fm_s", [128, NF], F32)
    K = A("K_s", [128, 8], F32)
    eye = A("eye_s", [128, 128], F32)
    eye_bf = A("eye_bf_s", [128, 128], BF16)
    ones_bf = A("ones_bf_s", [128, 1], BF16)
    colsums = A("colsums", [128, NF], F32)
    allred = A("allred", [128, NF], F32)
    qsc = A("qsc", [128, 2], F32)

    def wfs_slot(q):
        return sct[:, (q % 2) * 2048:(q % 2) * 2048 + 2048].bitcast(F32)
    C_EPS5, C_HALF, C_QRT, C_NQRT, C_EPS6, C_I1024, C_NEG1 = range(7)

    def KC(i):
        return K[:, i:i + 1]

    pb = {}
    for nm in ("pA1", "pA2", "pB", "pC", "pD", "pE", "pF", "pG"):
        pb[nm] = nc.alloc_psum_tensor(nm, [128, 512], F32)

    P = Prog()
    P.extra_sems = ["pollq", "exdq"]
    BK = Banks(P)
    for nm, t in pb.items():
        BK.reg(nm, t)

    for nm, prod, step in [
        ("dX", "sync", 16), ("dXR", "sync", 16), ("dOUT", "sync", 16),
        ("dI", "sync", 16), ("dW", "gpsimd", 16), ("dF", "sync", 16),
        ("Vmv", "vector", 1), ("Ssd", "scalar", 1), ("Vln", "vector", 1),
        ("TxnT", "tensor", 1), ("VxnE", "vector", 1),
        ("Tconv", "tensor", 1), ("Vxcb", "vector", 1),
        ("Tr", "tensor", 1), ("SaE", "scalar", 1),
        ("Ti", "tensor", 1), ("SiE", "scalar", 1), ("Vb1", "vector", 1),
        ("Vam", "vector", 1), ("Ssct", "scalar", 1),
        ("Vscan", "vector", 1), ("Vcar", "vector", 1), ("Vhbf", "vector", 1),
        ("Tffn", "tensor", 1), ("SfE", "scalar", 1), ("Vg", "vector", 1),
        ("Tw", "tensor", 1), ("SwoE", "scalar", 1), ("Ssq", "scalar", 1),
        ("Tones", "tensor", 1), ("Srms", "scalar", 1), ("Vrecip", "vector", 1),
        ("TrmsT", "tensor", 1), ("Vrtok", "vector", 1), ("Vt2", "vector", 1),
        ("TsgT", "tensor", 1), ("Vfin", "vector", 1),
        ("Vq", "vector", 1), ("Gq", "gpsimd", 1), ("Tq", "tensor", 1),
        ("Sln", "scalar", 1), ("dSH", "gpsimd", 16), ("dSR", "sync", 16),
        ("dEX", "gpsimd", 16), ("Gcc", "gpsimd", 1), ("Vhe", "vector", 1),
        ("Vht", "vector", 1), ("VAc", "vector", 1),
        ("Sinit", "scalar", 1),
        ("Vinit", "vector", 1),
    ]:
        P.sem(nm, prod, step)

    init_loads = [
        (K, consts_d), (eye, eye_d), (eye_bf, eyebf_d), (ones_bf, ones_d),
        (br2, br_d), (bi2, bi_d), (cb_fm, cb_d), (la_fm, la_d), (hgam, gam_d),
        (rmask, rmask_d), (slot_s, slot_d), (nonce_s, nonce_d),
    ]
    for sb, dr in init_loads:
        P.add("sync", lambda e, sb=sb, dr=dr: e.dma_start(out=sb[:, :], in_=dr[:, :]),
              inc="dI")
    m_dI_small = P.mark("dI")

    def wload(sb, dr):
        return P.add("gpsimd",
                     lambda e, sb=sb, dr=dr: e.dma_start(
                         out=bass.AP(sb, 0, [[NF * D, 128], [D, NF], [1, D]]),
                         in_=bass.AP(dr, 0, [[D, 128], [128 * D, NF], [1, D]])),
                     inc="dW")
    m_wr = wload(wr_bf, wrT_d)
    m_wi = wload(wi_bf, wiT_d)
    m_wo = wload(wo_bf, woT_d)
    m_conv = P.add("gpsimd",
                   lambda e: e.dma_start(
                       out=bass.AP(conv_bf, 0, [[32 * 128, 128], [128, 32], [1, 128]]),
                       in_=bass.AP(convd_d, 0, [[128, 128], [128 * 128, 32], [1, 128]])),
                   inc="dW")

    P.add("scalar",
          lambda e: e.activation(c4[:, :], la_fm[:, :], AF.Sigmoid),
          waits=[("dI", m_dI_small)], inc="Sinit")
    P.add("scalar",
          lambda e: e.activation(c4[:, :], c4[:, :], AF.Ln),
          inc="Sinit")
    m_sp = P.mark("Sinit")
    P.add("vector", lambda e: e.tensor_scalar_mul(c4[:, :], c4[:, :], 4.0),
          waits=[("Sinit", m_sp)], inc="Vinit")
    P.add("vector", lambda e: e.tensor_scalar_mul(br2[:, :], br2[:, :], 0.5),
          waits=[("dI", m_dI_small)], inc="Vinit")
    P.add("vector", lambda e: e.tensor_scalar_mul(bi2[:, :], bi2[:, :], 0.5),
          inc="Vinit")
    P.add("vector", lambda e: e.tensor_scalar_mul(hgam[:, :], hgam[:, :], 0.5),
          inc="Vinit")
    P.add("vector", lambda e: e.memset(zeros_t[:, :], 0.0), inc="Vinit")
    P.add("vector", lambda e: e.memset(gat[:, :], 0.0), inc="Vinit")
    m_vinit = P.mark("Vinit")

    mF = []
    for q in range(NF):
        mF.append(P.add("sync",
                        lambda e, q=q: e.dma_start(out=wfs_slot(q),
                                                   in_=wfT_d[q * 128:(q + 1) * 128, :]),
                        waits=[("Vq", q - 1)],
                        inc="dF"))
        P.add("vector",
              lambda e, q=q: e.tensor_reduce(colsums[:, q:q + 1], wfs_slot(q),
                                             mybir.AxisListType.X, ALU.add,
                                             apply_absolute_value=True),
              waits=[("dF", 16 * (q + 1))], inc="Vq")
    m_abs = P.mark("Vq")
    P.add("vector", lambda e: e.tensor_reduce(allred[:, 0:1], colsums[:, :],
                                              mybir.AxisListType.X, ALU.add),
          fwaits=[("Vq", m_abs)], inc="Vq")
    ww = BK.write_waits("pD")
    P.add("tensor",
          lambda e: e.transpose(pb["pD"][0:1, 0:128], allred[:, 0:1], eye[:, :]),
          waits=ww + [("Vq", P.mark("Vq")), ("dI", m_dI_small)], inc="Tq")
    BK.wrote("pD", "Tq", P.mark("Tq"))
    P.add("vector",
          lambda e: e.tensor_copy(rms_row[0:1, 0:128], pb["pD"][0:1, 0:128]),
          waits=BK.read_waits("pD"), inc="Vq")
    BK.read("pD", "Vq", P.mark("Vq"))
    P.add("vector", lambda e: e.tensor_reduce(st6[0:1, 0, 0:1], rms_row[0:1, 0:128],
                                              mybir.AxisListType.X, ALU.add),
          fwaits=[("Vq", P.mark("Vq"))], inc="Vq")
    P.add("vector", lambda e: e.tensor_copy(
        rms_row[0:1, 128:256], st6[0:1, 0, 0:1].to_broadcast((1, 128))),
          fwaits=[("Vq", P.mark("Vq"))], inc="Vq")
    ww = BK.write_waits("pD")
    P.add("tensor",
          lambda e: e.transpose(pb["pD"][:, 256:257], rms_row[0:1, 128:256],
                                eye[0:1, 0:1]),
          waits=ww + [("Vq", P.mark("Vq"))], inc="Tq")
    BK.wrote("pD", "Tq", P.mark("Tq"))
    P.add("vector", lambda e: e.tensor_copy(allred[:, 1:2], pb["pD"][:, 256:257]),
          waits=BK.read_waits("pD"), inc="Vq")
    BK.read("pD", "Vq", P.mark("Vq"))
    P.add("vector", lambda e: e.tensor_scalar_mul(qsc[:, 0:1], allred[:, 1:2],
                                                  1.0 / (D * D)),
          fwaits=[("Vq", P.mark("Vq"))], inc="Vq")
    P.add("vector", lambda e: e.tensor_scalar_max(qsc[:, 0:1], qsc[:, 0:1], 1e-5),
          fwaits=[("Vq", P.mark("Vq"))], inc="Vq")
    P.add("vector", lambda e: e.reciprocal(qsc[:, 1:2], qsc[:, 0:1]),
          fwaits=[("Vq", P.mark("Vq"))], inc="Vq")
    m_qsc = P.mark("Vq")
    qt, q1, q2 = a_t[:, 0:D], a_t[:, D:2 * D], a_t[:, 2 * D:3 * D]
    for q in range(NF):
        P.add("sync",
              lambda e, q=q: e.dma_start(out=wfs_slot(q),
                                         in_=wfT_d[q * 128:(q + 1) * 128, :]),
              waits=[("Vq", m_qsc if q < 2 else m_qsc + 5 * (q - 1))],
              inc="dF")
        P.add("vector",
              lambda e, q=q: e.tensor_scalar_mul(qt, wfs_slot(q), qsc[:, 1:2]),
              waits=[("dF", 16 * (NF + q + 1))],
              fwaits=[("Vq", m_qsc)], inc="Vq")
        P.add("vector",
              lambda e: e.tensor_single_scalar(q1, qt, 0.5, ALU.is_ge), inc="Vq")
        P.add("vector",
              lambda e: e.tensor_single_scalar(q2, qt, -0.5, ALU.is_le), inc="Vq")
        P.add("vector", lambda e: e.tensor_sub(q1, q1, q2), inc="Vq")
        P.add("vector",
              lambda e, q=q: e.tensor_scalar_mul(wf_bf[:, q * D:(q + 1) * D], q1,
                                                 qsc[:, 0:1]),
              inc="Vq")
    m_quant = P.mark("Vq")

    m_ln = {}
    m_fin = {}
    m_dout = {}
    m_dxr = {}
    m_scanC = {}
    m_bmul = {}
    m_car = {}
    m_acar = {}
    m_xnE = {}
    m_xcb = {}
    m_convAll = {}
    m_store = {}
    m_gstore = {}
    m_read = {}
    m_ht = {}

    rotbank_fr = ["pD", "pE"]
    fr_idx = [0]

    def next_fr():
        b = rotbank_fr[fr_idx[0] % 2]
        fr_idx[0] += 1
        return b

    def OH(c0, c1):
        return sct[:, 2 * c0:2 * c1].bitcast(F32)

    def h_slot(p):
        return a_t[:, p * 2048:(p + 1) * 2048].bitcast(BF16)

    def A_slot(p):
        return b_t[:, p * 2048:(p + 1) * 2048].bitcast(BF16)

    def g_slot(p, lo=0, hi=NF * TC):
        return h_t[:, p * 2048 + lo // 2: p * 2048 + hi // 2].bitcast(BF16)

    P.add("sync", lambda e: e.dma_start(out=x_ln[0:3, 1, :], in_=x_d[0:3, :]),
          inc="dI")
    m_dlead = P.mark("dI")
    P.add("vector", lambda e: e.bn_stats(st6[0:3, 1, 0:6], x_ln[0:3, 1, 0:512]),
          waits=[("dI", m_dlead)], inc="Vmv")
    P.add("vector", lambda e: e.bn_stats(st6[0:3, 1, 6:12], x_ln[0:3, 1, 512:1024]),
          inc="Vmv")
    m_lst = P.mark("Vmv")
    P.add("vector", lambda e: e.bn_aggr(mv[0:3, 0, :], st6[0:3, 1, 0:12]),
          fwaits=[("Vmv", m_lst)], inc="Vmv")
    m_lag = P.mark("Vmv")
    P.add("scalar", lambda e: e.activation(sd_t[0:3, 0:1], mv[0:3, 0, 1:2],
                                           AF.Sqrt, bias=K[0:3, C_EPS5:C_EPS5 + 1]),
          waits=[("Vmv", m_lag), ("dI", m_dI_small)], inc="Ssd")
    m_lsd = P.mark("Ssd")
    P.add("vector", lambda e: e.reciprocal(rs_t[0:3, 0:1], sd_t[0:3, 0:1]),
          waits=[("Ssd", m_lsd)], inc="Vln")
    m_lrs = P.mark("Vln")
    P.add("vector", lambda e: e.scalar_tensor_tensor(nb_t[0:3, 0:1], mv[0:3, 0, 0:1],
                                                     -1.0, rs_t[0:3, 0:1],
                                                     ALU.mult, ALU.mult),
          fwaits=[("Vln", m_lrs)], inc="Vln")
    m_lnb = P.mark("Vln")
    P.add("scalar", lambda e: e.activation(g_t[0:3, 0:1024], x_ln[0:3, 1, :],
                                           AF.Identity, bias=nb_t[0:3, 0:1],
                                           scale=rs_t[0:3, 0:1]),
          waits=[("Vln", m_lnb)], inc="Sln")
    m_lead = P.mark("Sln")
    for j in range(NF):
        bank = "pA1" if j % 2 == 0 else "pA2"
        ww = BK.write_waits(bank)
        P.add("tensor",
              lambda e, j=j, bank=bank: e.transpose(
                  BK.t(bank)[:, 0:2].bitcast(BF16)[:, 0:3],
                  g_t[0:3, j * 128:(j + 1) * 128],
                  eye_bf[0:3, 0:3]),
              waits=ww + [("Sln", m_lead), ("dI", m_dI_small)], inc="TxnT")
        BK.wrote(bank, "TxnT", P.mark("TxnT"))
        P.add("vector",
              lambda e, j=j, bank=bank: e.tensor_copy(
                  xn_feat[:, j, 0:3], BK.t(bank)[:, 0:2].bitcast(BF16)[:, 0:3]),
              waits=BK.read_waits(bank), inc="VxnE")
        BK.read(bank, "VxnE", P.mark("VxnE"))

    def emit_lnload(k):
        for t in range(NTT):
            g = 4 * k + t
            xw = [("Sln", m_ln.get(g - 2, 0))]
            if g == 1:
                xw.append(("Sln", m_lead))
            P.add("sync",
                  lambda e, g=g: e.dma_start(
                      out=x_ln[:, g % 2, :],
                      in_=x_d[3 + g * 128:3 + (g + 1) * 128, :]),
                  waits=xw, inc="dX")
            P.add("vector",
                  lambda e, t=t, g=g: e.bn_stats(st6[:, t % 2, 0:6],
                                                 x_ln[:, g % 2, 0:512]),
                  waits=[("dX", 16 * (g + 1))], inc="Vmv")
            P.add("vector",
                  lambda e, t=t, g=g: e.bn_stats(st6[:, t % 2, 6:12],
                                                 x_ln[:, g % 2, 512:1024]),
                  inc="Vmv")
            m_st = P.mark("Vmv")
            P.add("vector",
                  lambda e, t=t: e.bn_aggr(mv[:, t, :], st6[:, t % 2, 0:12]),
                  fwaits=[("Vmv", m_st)], inc="Vmv")
            m_ag = P.mark("Vmv")
            P.add("scalar",
                  lambda e, t=t: e.activation(sd_t[:, t:t + 1], mv[:, t, 1:2],
                                              AF.Sqrt, bias=KC(C_EPS5)),
                  waits=[("Vmv", m_ag)], inc="Ssd")
            m_sd = P.mark("Ssd")
            P.add("vector",
                  lambda e, t=t: e.reciprocal(rs_t[:, t:t + 1], sd_t[:, t:t + 1]),
                  waits=[("Ssd", m_sd)], inc="Vln")
            m_rs = P.mark("Vln")
            P.add("vector",
                  lambda e, t=t: e.scalar_tensor_tensor(nb_t[:, t:t + 1],
                                                        mv[:, t, 0:1], -1.0,
                                                        rs_t[:, t:t + 1],
                                                        ALU.mult, ALU.mult),
                  fwaits=[("Vln", m_rs)], inc="Vln")
            m_nb = P.mark("Vln")
            m_ln[g] = P.add("scalar",
                            lambda e, g=g, t=t: e.activation(
                                xn_tok[:, t * D:(t + 1) * D], x_ln[:, g % 2, :],
                                AF.Identity, bias=nb_t[:, t:t + 1],
                                scale=rs_t[:, t:t + 1]),
                            waits=[("TxnT", P.mark("TxnT")), ("Vln", m_nb)],
                            inc="Sln")

    def emit_front(k):
        if k > 0:
            P.add("vector",
                  lambda e: e.tensor_copy(xn_feat[:, :, 0:3], xn_feat[:, :, 512:515]),
                  waits=[("Tconv", m_convAll[k - 1])], inc="Vln")

        for j in range(NF):
            bank = "pA1" if j % 2 == 0 else "pA2"
            ww = BK.write_waits(bank)
            for t in range(NTT):
                P.add("tensor",
                      lambda e, j=j, t=t, bank=bank: e.transpose(
                          BK.t(bank)[:, 0:256].bitcast(BF16)[:, t * 128:(t + 1) * 128],
                          xn_tok[:, t * D + j * 128: t * D + (j + 1) * 128],
                          eye_bf[:, :]),
                      waits=(ww if t == 0 else []) + [("Sln", m_ln[4 * k + NTT - 1])],
                      inc="TxnT" if t == NTT - 1 else None)
            BK.wrote(bank, "TxnT", P.mark("TxnT"))
            m_xnE[(k, j)] = P.add(
                "vector",
                lambda e, j=j, bank=bank: e.tensor_copy(
                    xn_feat[:, j, 3:515], BK.t(bank)[:, 0:256].bitcast(BF16)),
                waits=BK.read_waits(bank) + [("Vinit", m_vinit),
                                             ("Tffn", P.mark("Tffn"))],
                inc="VxnE")
            BK.read(bank, "VxnE", m_xnE[(k, j)])

        for j in range(NF):
            bank = "pB" if j % 2 == 0 else "pC"
            ww = BK.write_waits(bank)
            for tap in range(4):
                P.add("tensor",
                      lambda e, j=j, tap=tap, bank=bank: e.matmul(
                          BK.t(bank)[:, 0:512],
                          conv_bf[:, (j * 4 + tap) * 128:(j * 4 + tap + 1) * 128],
                          xn_feat[:, j, tap:tap + 512],
                          start=(tap == 0), stop=(tap == 3)),
                      waits=(ww if tap == 0 else []) + [("VxnE", m_xnE[(k, j)]),
                                                        ("dW", m_conv)],
                      inc="Tconv" if tap == 3 else None)
            BK.wrote(bank, "Tconv", P.mark("Tconv"))
            if j % 2 == 0:
                m_xcb[(k, j)] = P.add(
                    "scalar",
                    lambda e, j=j, bank=bank: e.activation(
                        xcb[:, j * TC:(j + 1) * TC], BK.t(bank)[:, 0:512],
                        AF.Identity, bias=cb_fm[:, j:j + 1]),
                    waits=BK.read_waits(bank), inc="Vxcb", )
                BK.read(bank, "Vxcb", m_xcb[(k, j)])
            else:
                m_xcb[(k, j)] = P.add(
                    "vector",
                    lambda e, j=j, bank=bank: e.tensor_scalar(
                        xcb[:, j * TC:(j + 1) * TC], BK.t(bank)[:, 0:512],
                        cb_fm[:, j:j + 1], None, ALU.add),
                    waits=BK.read_waits(bank) + [("Ti", P.mark("Ti"))], inc="Vxcb")
                BK.read(bank, "Vxcb", m_xcb[(k, j)])
        m_convAll[k] = P.mark("Tconv")
        m_xcb_all = P.mark("Vxcb")

        for j in range(NF):
            bank = next_fr()
            ww = BK.write_waits(bank)
            for q in range(NF):
                P.add("tensor",
                      lambda e, j=j, q=q, bank=bank: e.matmul(
                          BK.t(bank)[:, 0:512],
                          wr_bf[:, q * D + j * 128: q * D + (j + 1) * 128],
                          xcb[:, q * TC:(q + 1) * TC],
                          start=(q == 0), stop=(q == NF - 1)),
                      waits=(ww if q == 0 else []) + [("Vxcb", m_xcb_all), ("dW", m_wr)],
                      inc="Tr" if q == NF - 1 else None)
            BK.wrote(bank, "Tr", P.mark("Tr"))
            P.add("scalar",
                  lambda e, j=j, bank=bank: e.activation(
                      rt_rot[:, :], BK.t(bank)[:, 0:512], AF.Tanh,
                      bias=br2[:, j:j + 1], scale=KC(C_HALF)),
                  waits=BK.read_waits(bank) + [("Vinit", m_vinit)], inc="SaE")
            BK.read(bank, "SaE", P.mark("SaE"))
            P.add("scalar",
                  lambda e, j=j: e.activation(
                      a_t[:, j * TC:(j + 1) * TC], rt_rot[:, :], AF.Exp,
                      bias=c4[:, j:j + 1], scale=c4[:, j:j + 1]),
                  waits=[("VAc", m_acar.get(k - 1, 0)),
                         ("Vq", m_quant if k == 0 else 0)],
                  inc="SaE")
        m_aAll = P.mark("SaE")

        for j in range(NF):
            bank = next_fr()
            ww = BK.write_waits(bank)
            for q in range(NF):
                P.add("tensor",
                      lambda e, j=j, q=q, bank=bank: e.matmul(
                          BK.t(bank)[:, 0:512],
                          wi_bf[:, q * D + j * 128: q * D + (j + 1) * 128],
                          xcb[:, q * TC:(q + 1) * TC],
                          start=(q == 0), stop=(q == NF - 1)),
                      waits=(ww if q == 0 else []) + [("dW", m_wi)],
                      inc="Ti" if q == NF - 1 else None)
            BK.wrote(bank, "Ti", P.mark("Ti"))
            P.add("scalar",
                  lambda e, j=j, bank=bank: e.activation(
                      it_rot[:, j % 2, :], BK.t(bank)[:, 0:512], AF.Tanh,
                      bias=bi2[:, j:j + 1], scale=KC(C_HALF)),
                  waits=BK.read_waits(bank) + [("Vb1", max(0, P.mark("Vb1") - 1))],
                  inc="SiE")
            BK.read(bank, "SiE", P.mark("SiE"))
            m_sie = P.mark("SiE")
            P.add("vector",
                  lambda e, j=j: e.scalar_tensor_tensor(
                      b_t[:, j * TC:(j + 1) * TC], it_rot[:, j % 2, :], 1.0,
                      xcb[:, j * TC:(j + 1) * TC], ALU.add, ALU.mult),
                  waits=[("SiE", m_sie)], inc="Vb1")
        m_b1All = P.mark("Vb1")

        P.add("vector",
              lambda e: e.tensor_scalar_min(a_t[:, :], a_t[:, :], 1.0),
              waits=[("SaE", m_aAll)], inc="Vam")
        m_am = P.mark("Vam")
        P.add("scalar",
              lambda e: e.activation(h_t[:, :], a_t[:, :], AF.Square),
              waits=[("Vam", m_am), ("dSH", m_store.get(k - 1, (0, 0))[1]),
                     ("Vcar", m_car.get(k - 1, 0))],
              inc="Ssct")
        P.add("scalar",
              lambda e: e.activation(sct[:, :], h_t[:, :], AF.Sqrt,
                                     bias=KC(C_QRT), scale=KC(C_NQRT)),
              waits=[("Vb1", m_bmul.get(k - 1, 0))],
              inc="Ssct")
        m_sct = P.mark("Ssct")
        P.add("vector",
              lambda e: e.tensor_mul(b_t[:, :], b_t[:, :], sct[:, :]),
              waits=[("Ssct", m_sct), ("Vb1", m_b1All)], inc="Vb1")
        m_bmul[k] = P.mark("Vb1")
        for j in range(NF):
            init_h = 0.0 if k == 0 else carry[:, j:j + 1]
            P.add("vector",
                  lambda e, j=j, init_h=init_h: e.tensor_tensor_scan(
                      h_t[:, j * TC:(j + 1) * TC], a_t[:, j * TC:(j + 1) * TC],
                      b_t[:, j * TC:(j + 1) * TC], init_h, ALU.mult, ALU.add),
                  waits=[("Ssct", m_sct)], inc="Vscan")
        m_scanC[k] = P.mark("Vscan")
        m_car[k] = P.add("vector",
                         lambda e: e.tensor_copy(
                             bass.AP(carry, 0, [[NF, 128], [1, NF], [1, 1]]),
                             bass.AP(h_t, TC - 1, [[NF * TC, 128], [TC, NF], [1, 1]])),
                         fwaits=[("Vscan", m_scanC[k])], inc="Vcar")
        for j in range(NF):
            init_a = 1.0 if k == 0 else acar[:, j:j + 1]
            P.add("vector",
                  lambda e, j=j, init_a=init_a: e.tensor_tensor_scan(
                      A_t[:, j * TC:(j + 1) * TC], a_t[:, j * TC:(j + 1) * TC],
                      zeros_t[:, :], init_a, ALU.mult, ALU.add),
                  waits=([("dSH", m_store.get(k - 1, (0, 0))[1])] if j == 0 else []),
                  inc="Vscan")
        m_acar[k] = P.add("vector",
                          lambda e: e.tensor_copy(
                              bass.AP(acar, 0, [[NF, 128], [1, NF], [1, 1]]),
                              bass.AP(A_t, TC - 1, [[NF * TC, 128], [TC, NF], [1, 1]])),
                          fwaits=[("Vscan", P.mark("Vscan"))], inc="VAc")
        P.add("gpsimd", lambda e, k=k: e.dma_start(out=h_scr[k, :, :], in_=h_t[:, :]),
              waits=[("Vcar", m_car[k])], inc="dSH")
        P.add("gpsimd", lambda e, k=k: e.dma_start(out=a_scr[k, :, :], in_=A_t[:, :]),
              waits=[("VAc", m_acar[k])], inc="dSH")
        m_store[k] = (P.mark("Vhbf"), P.mark("dSH"))

        for j in range(NF):
            bank = next_fr()
            ww = BK.write_waits(bank)
            for q in range(NF):
                P.add("tensor",
                      lambda e, j=j, q=q, bank=bank: e.matmul(
                          BK.t(bank)[:, 0:512],
                          wf_bf[:, q * D + j * 128: q * D + (j + 1) * 128],
                          xn_feat[:, q, 3:515],
                          start=(q == 0), stop=(q == NF - 1)),
                      waits=(ww if q == 0 else []) + [("Vq", m_quant),
                                                      ("VxnE", m_xnE[(k, NF - 1)])],
                      inc="Tffn" if q == NF - 1 else None)
            BK.wrote(bank, "Tffn", P.mark("Tffn"))
            P.add("scalar",
                  lambda e, j=j, bank=bank: e.activation(
                      tt_rot[:, j % 2, :], BK.t(bank)[:, 0:512], AF.Tanh,
                      scale=KC(C_HALF)),
                  waits=BK.read_waits(bank) + [("Vg", max(0, P.mark("Vg") - 1))],
                  inc="SfE")
            P.add("scalar",
                  lambda e, j=j, bank=bank: e.activation(
                      ff_rot[:, j % 2, :], BK.t(bank)[:, 0:512], AF.Identity,
                      scale=hgam[:, j:j + 1]),
                  inc="SfE")
            BK.read(bank, "SfE", P.mark("SfE"))
            m_fe = P.mark("SfE")
            P.add("vector",
                  lambda e, j=j: e.scalar_tensor_tensor(
                      g_t[:, j * TC:(j + 1) * TC], tt_rot[:, j % 2, :], 1.0,
                      ff_rot[:, j % 2, :], ALU.add, ALU.mult),
                  waits=[("SfE", m_fe), ("dSH", m_gstore.get(k - 1, 0))], inc="Vg")
        mg = P.mark("Vg")
        P.add("gpsimd", lambda e, k=k: e.dma_start(out=g_scr[k, :, :], in_=g_t[:, :]),
              waits=[("Vg", mg)], inc="dSH")
        m_gstore[k] = P.mark("dSH")

    def emit_exchange():
        def ex_fn(e):
            rs_r = e.alloc_register("exslot")
            rp_r = e.alloc_register("expslot")
            rn = e.alloc_register("exnonce")
            rf = e.alloc_register("exflag")
            rv = e.alloc_register("expollv")
            e.reg_load(rs_r, slot_s[0:1, 0:1])
            e.reg_load(rp_r, slot_s[0:1, 1:2])
            e.reg_load(rn, nonce_s[0:1, 0:1])
            rs = e.snap(rs_r, min_val=0, max_val=1)
            rp = e.snap(rp_r, min_val=0, max_val=1)
            pq = P.handles["pollq"]
            dq = P.handles["exdq"]
            e.dma_start(out=sh_x[bass.ds(rs, 1), :, :],
                        in_=carry[:, :]).then_inc(dq, 16)
            e.wait_ge(dq, 16)
            e.dma_start(out=sh_f[bass.ds(rs, 1), 0:1],
                        in_=nonce_s[0:1, :]).then_inc(dq, 16)
            e.wait_ge(dq, 32)
            rneed = e.alloc_register("exneed")
            e.reg_load(rneed, slot_s[0:1, 2:3])
            e.reg_mul(rn, rn, rneed)
            e.reg_mov(rf, 0)
            e.reg_mov(rv, 0)
            with e.While(lambda: e.snap(rf, donate=False)
                         != e.snap(rn, donate=False)):
                e.reg_add(rv, rv, 16)
                e.dma_start(out=flag_s[0:1, :],
                            in_=sh_f[bass.ds(rp, 1), 0:1]).then_inc(pq, 16)
                e.wait_ge(pq, rv)
                e.reg_load(rf, flag_s[0:1, 0:1])
            nv = e.snap(rneed, min_val=0, max_val=1)
            e.dma_start(out=gat[:, :],
                        in_=sh_x[bass.ds(rp, 1), :, :], cond=nv).then_inc(dq, 16)
            e.wait_ge(dq, 48)
            return e.nop()

        P.add("gpsimd", ex_fn,
              waits=[("Vcar", m_car[n_chunks - 1]), ("dI", m_dI_small)],
              inc="Gcc")
        P.add("vector", lambda e: e.tensor_mul(hent[:, :], gat[:, :], rmask[:, :]),
              waits=[("Gcc", 1)], inc="Vhe")

    m_vhe_done = [0]
    p1_final = []

    def emit_read(k):
        if k >= n_chunks:
            return
        p = k % 2
        w = [("dSH", m_store[k][1]), ("Vht", m_ht.get(k - 2, 0)),
             ("Tw", P.mark("Tw")), ("Vt2", P.mark("Vt2"))] + p1_final
        P.add("sync", lambda e, k=k, p=p: e.dma_start(out=h_slot(p), in_=h_scr[k, :, :]),
              waits=w, inc="dSR")
        P.add("sync", lambda e, k=k, p=p: e.dma_start(out=A_slot(p), in_=a_scr[k, :, :]),
              waits=[("dSH", m_store[k][1])] + p1_final, inc="dSR")
        P.add("sync", lambda e, k=k, p=p: e.dma_start(out=g_slot(p), in_=g_scr[k, :, :]),
              waits=[("dSH", m_gstore[k])] + p1_final, inc="dSR")
        m_read[k] = P.mark("dSR")

    def emit_back(k):
        p = k % 2
        for j in range(NF):
            P.add("vector",
                  lambda e, j=j, p=p: e.scalar_tensor_tensor(
                      xcb[:, j * TC:(j + 1) * TC],
                      A_slot(p)[:, j * TC:(j + 1) * TC],
                      hent[:, j:j + 1],
                      h_slot(p)[:, j * TC:(j + 1) * TC],
                      ALU.mult, ALU.add),
                  waits=[("dSR", m_read[k]),
                         ("Tw", P.mark("Tw"))] + p1_final,
                  fwaits=[("Vhe", m_vhe_done[0])],
                  inc="Vht")
        m_ht[k] = P.mark("Vht")

        for hh in range(2):
            for j in range(NF):
                bank = "pF" if j % 2 == 0 else "pG"
                ww = BK.write_waits(bank)
                for q in range(NF):
                    P.add("tensor",
                          lambda e, j=j, q=q, hh=hh, bank=bank: e.matmul(
                              BK.t(bank)[:, 0:256],
                              wo_bf[:, q * D + j * 128: q * D + (j + 1) * 128],
                              xcb[:, q * TC + hh * 256: q * TC + hh * 256 + 256],
                              start=(q == 0), stop=(q == NF - 1)),
                          waits=(ww if q == 0 else []) + [("Vht", m_ht[k]),
                                                          ("dW", m_wo)],
                          inc="Tw" if q == NF - 1 else None)
                BK.wrote(bank, "Tw", P.mark("Tw"))
                P.add("scalar",
                      lambda e, j=j, bank=bank: e.activation(
                          OH(j * 256, (j + 1) * 256), BK.t(bank)[:, 0:256],
                          AF.Identity),
                      waits=BK.read_waits(bank) + [("TsgT", P.mark("TsgT"))] + p1_final,
                      inc="SwoE")
                BK.read(bank, "SwoE", P.mark("SwoE"))
                P.add("scalar",
                      lambda e, j=j: e.activation(
                          sq_rot[:, j % 2, :], OH(j * 256, (j + 1) * 256),
                          AF.Square),
                      inc="Ssq")
                m_sq = P.mark("Ssq")
                wwB = BK.write_waits("pB") if j == 0 else []
                P.add("tensor",
                      lambda e, j=j, hh=hh: e.matmul(
                          pb["pB"][0:1, hh * 256:hh * 256 + 256],
                          ones_bf[:, 0:1], sq_rot[:, j % 2, :],
                          start=(j == 0), stop=(j == NF - 1)),
                      waits=wwB + [("Ssq", m_sq)],
                      inc="Tones" if j == NF - 1 else None)
            BK.wrote("pB", "Tones", P.mark("Tones"))
            P.add("scalar",
                  lambda e, hh=hh: e.activation(
                      rms_row[0:1, :], pb["pB"][0:1, hh * 256:hh * 256 + 256],
                      AF.Sqrt, bias=K[0:1, C_EPS6:C_EPS6 + 1],
                      scale=K[0:1, C_I1024:C_I1024 + 1]),
                  waits=BK.read_waits("pB") + [("Vrecip", P.mark("Vrecip")),
                                               ("TrmsT", P.mark("TrmsT"))],
                  inc="Srms")
            BK.read("pB", "Srms", P.mark("Srms"))
            m_srms = P.mark("Srms")
            P.add("vector", lambda e: e.reciprocal(rms_row[0:1, :], rms_row[0:1, :]),
                  waits=[("Srms", m_srms)], inc="Vrecip")
            m_rc = P.mark("Vrecip")
            for tt in range(2):
                tile = hh * 2 + tt
                ww = BK.write_waits("pC")
                P.add("tensor",
                      lambda e, tt=tt, tile=tile: e.transpose(
                          pb["pC"][:, 384 + tile:384 + tile + 1],
                          rms_row[0:1, tt * 128:(tt + 1) * 128],
                          eye[0:1, 0:1]),
                      waits=ww + [("Vrecip", m_rc)], inc="TrmsT")
                BK.wrote("pC", "TrmsT", P.mark("TrmsT"))
                m_tt = P.mark("TrmsT")
                P.add("vector",
                      lambda e, tile=tile: e.tensor_copy(
                          rms_tok[:, tile:tile + 1],
                          pb["pC"][:, 384 + tile:384 + tile + 1]),
                      waits=[("TrmsT", m_tt)], inc="Vrtok")
                BK.read("pC", "Vrtok", P.mark("Vrtok"))
            for j in range(NF):
                P.add("vector",
                      lambda e, j=j, hh=hh, p=p: e.tensor_tensor(
                          OH(j * 256, (j + 1) * 256),
                          OH(j * 256, (j + 1) * 256),
                          g_slot(p, j * TC + hh * 256, j * TC + hh * 256 + 256),
                          ALU.mult),
                      waits=[("SwoE", P.mark("SwoE")), ("dSR", m_read[k])],
                      inc="Vt2")
            m_t2 = P.mark("Vt2")
            for tt in range(2):
                tile = hh * 2 + tt
                g = 4 * k + tile
                pair = ("pA1", "pA2") if tt == 0 else ("pD", "pE")
                P.add("sync",
                      lambda e, g=g: e.dma_start(
                          out=x_ln[:, g % 2, :],
                          in_=x_d[3 + g * 128:3 + (g + 1) * 128, :]),
                      waits=[("dOUT", m_dout.get(g - 2, 0)),
                             ("Sln", m_ln[4 * n_chunks - 1])], inc="dXR")
                m_dxr[g] = P.mark("dXR")
                ww0 = BK.write_waits(pair[0])
                ww1 = BK.write_waits(pair[1])
                for j in range(NF):
                    bk = pair[0] if j < 4 else pair[1]
                    P.add("tensor",
                          lambda e, j=j, tt=tt, bk=bk: e.transpose(
                              BK.t(bk)[:, (j % 4) * 128:(j % 4) * 128 + 128],
                              OH(j * 256 + tt * 128, j * 256 + tt * 128 + 128),
                              eye[:, :]),
                          waits=(ww0 + ww1 if j == 0 else []) + [("Vt2", m_t2)],
                          inc="TsgT" if j == NF - 1 else None)
                BK.wrote(pair[0], "TsgT", P.mark("TsgT"))
                BK.wrote(pair[1], "TsgT", P.mark("TsgT"))
                m_sgt = P.mark("TsgT")
                for half_pb in range(2):
                    bk = pair[half_pb]
                    P.add("vector",
                          lambda e, g=g, tile=tile, bk=bk, half_pb=half_pb:
                          e.scalar_tensor_tensor(
                              x_ln[:, g % 2, half_pb * 512:(half_pb + 1) * 512],
                              BK.t(bk)[:, 0:512],
                              rms_tok[:, tile:tile + 1],
                              x_ln[:, g % 2, half_pb * 512:(half_pb + 1) * 512],
                              ALU.mult, ALU.add),
                          waits=[("TsgT", m_sgt), ("dXR", m_dxr[g])],
                          fwaits=[("Vrtok", P.mark("Vrtok"))],
                          inc="Vfin")
                    BK.read(bk, "Vfin", P.mark("Vfin"))
                m_fin[g] = P.mark("Vfin")
                m_dout[g] = P.add("sync",
                                  lambda e, g=g: e.dma_start(
                                      out=out_d[g * 128:(g + 1) * 128, :],
                                      in_=x_ln[:, g % 2, :]),
                                  waits=[("Vfin", m_fin[g])], inc="dOUT")

    emit_lnload(0)
    for k in range(n_chunks):
        emit_front(k)
        if k + 1 < n_chunks:
            emit_lnload(k + 1)
    p1_final.extend([("Vscan", P.mark("Vscan")), ("VAc", P.mark("VAc")),
                     ("Vcar", P.mark("Vcar")), ("dSH", P.mark("dSH")),
                     ("Vb1", P.mark("Vb1")), ("Tffn", P.mark("Tffn")),
                     ("Ssct", P.mark("Ssct"))])
    emit_exchange()
    m_vhe_done[0] = P.mark("Vhe")
    emit_read(0)
    emit_read(1)
    for k in range(n_chunks):
        emit_back(k)
        emit_read(k + 2)

    if debug:
        dbg = {}
        for nm, t in [("a", a_t), ("b", b_t), ("h", h_t), ("xcb", xcb),
                      ("xntok", xn_tok), ("mvd", mv),
                      ("carryd", carry), ("acard", acar), ("hentd", hent),
                      ("gatd", gat), ("At", A_t),
                      ("g", g_t), ("xnf", xn_feat), ("wfq", wf_bf),
                      ("c4d", c4), ("qscd", qsc), ("rmstokd", rms_tok)]:
            shp = list(t.shape)
            flat = [shp[0], int(np.prod(shp[1:]))]
            dt_ = t.dtype
            od = nc.dram_tensor("dbg_" + nm, flat, dt_, kind="ExternalOutput")
            dbg[nm] = (t, od, flat)
        wl = [(nm, 16 * P.sems[nm]["c"] if P.sems[nm]["step"] == 16 else P.sems[nm]["c"])
              for nm in P.sems]
        wl = [(nm, P.sems[nm]["c"]) for nm in P.sems]
        for nm, (t, od, flat) in dbg.items():
            P.add("sync",
                  lambda e, t=t, od=od, flat=flat: e.dma_start(
                      out=od[:, :],
                      in_=bass.AP(t, 0, [[flat[1], flat[0]], [1, flat[1]]])),
                  waits=[("Vfin", P.mark("Vfin")), ("dOUT", P.mark("dOUT"))],
                  inc="dOUT")

    P.add("sync", lambda e: e.nop(),
          waits=[("dOUT", P.mark("dOUT"))])

    P.emit(nc)
    return nc


_NC_CACHE = {}


def _get_nc(n_chunks):
    if n_chunks not in _NC_CACHE:
        _NC_CACHE[n_chunks] = build(n_chunks)
    return _NC_CACHE[n_chunks]


def make_inputs(x_full, conv_w, conv_b, W_r, b_r, W_i, b_i, log_a, W_out, gamma,
                W_ffn):
    def fm(v):
        return np.ascontiguousarray(v.reshape(NF, 128).T).astype(np.float32)

    convd = np.zeros((32, 128, 128), np.float32)
    cw = np.asarray(conv_w).reshape(D, 4)
    for j in range(NF):
        for kk in range(4):
            np.fill_diagonal(convd[j * 4 + kk], cw[j * 128:(j + 1) * 128, kk])

    consts = np.zeros((128, 8), np.float32)
    consts[:, 0] = 1e-5
    consts[:, 1] = 0.5
    consts[:, 2] = 0.25
    consts[:, 3] = -0.25
    consts[:, 4] = 1e-6
    consts[:, 5] = 1.0 / D
    consts[:, 6] = -1.0

    shared = dict(
        wrT=np.ascontiguousarray(np.asarray(W_r).T),
        wiT=np.ascontiguousarray(np.asarray(W_i).T),
        woT=np.ascontiguousarray(np.asarray(W_out).T),
        wfT=np.ascontiguousarray(np.asarray(W_ffn).T),
        convd=convd,
        br_fm=fm(np.asarray(b_r)), bi_fm=fm(np.asarray(b_i)),
        cb_fm=fm(np.asarray(conv_b)), la_fm=fm(np.asarray(log_a)),
        gam_fm=fm(np.asarray(gamma)),
        consts=consts,
        eye=np.eye(128, dtype=np.float32),
        eye_bf=np.eye(128, dtype=ml_dtypes.bfloat16),
        ones_bf=np.ones((128, 1), ml_dtypes.bfloat16),
    )
    x = np.asarray(x_full)
    B, L, d = x.shape
    half = L // 2
    nonce = np.array([[np.random.randint(1, 2**31)]], np.uint32)
    in_maps = []
    for c in range(8):
        b, h = c // 2, c % 2
        m = dict(shared)
        if h == 0:
            xh = np.concatenate([np.zeros((3, d), np.float32), x[b, 0:half]], axis=0)
        else:
            xh = x[b, half - 3:L]
        m["x"] = np.ascontiguousarray(xh)
        rm = np.zeros((128, NF), np.float32)
        if h == 1:
            rm[:, :] = 1.0
        m["rmask"] = rm
        m["slot"] = np.array([[h, 1 - h, h, 0]], np.uint32)
        m["nonce"] = nonce
        in_maps.append(m)
    return in_maps


def kernel(L0_attn_x, L0_attn_conv_w, L0_attn_conv_b, L0_attn_W_r, L0_attn_b_r,
           L0_attn_W_i, L0_attn_b_i, L0_attn_log_a, L0_attn_W_out, L0_attn_gamma,
           L0_ffn_weight):
    x = np.asarray(L0_attn_x, dtype=np.float32)
    B, L, d = x.shape
    n_chunks = L // 2 // TC
    nc = _get_nc(n_chunks)
    in_maps = make_inputs(x, L0_attn_conv_w, L0_attn_conv_b, L0_attn_W_r,
                          L0_attn_b_r, L0_attn_W_i, L0_attn_b_i, L0_attn_log_a,
                          L0_attn_W_out, L0_attn_gamma, L0_ffn_weight)
    res = run_bass_kernel_spmd(nc, in_maps, core_ids=list(range(8)))
    half = L // 2
    out = np.empty((B, L, d), np.float32)
    for c in range(8):
        b, h = c // 2, c % 2
        out[b, h * half:(h + 1) * half] = res.results[c]["out"]
    return out


# version 05
# speedup vs baseline: 1.0184x; 1.0184x over previous
import numpy as np
import ml_dtypes
import concourse.bass as bass
import concourse.mybir as mybir
from concourse.bass_utils import run_bass_kernel_spmd
from concourse.bass_isa import ReduceOp

F32 = mybir.dt.float32
BF16 = mybir.dt.bfloat16
AF = mybir.ActivationFunctionType
ALU = mybir.AluOpType

D = 1024
NF = 8
TC = 512
NTT = TC // 128

ENGS = ("sync", "gpsimd", "vector", "scalar", "tensor")


class Prog:

    def __init__(self):
        self.ops = {e: [] for e in ENGS}
        self.sems = {}
        self.prod_at = {e: {} for e in ENGS}
        self.waited = {e: {} for e in ENGS}

    def sem(self, name, producer, step=1):
        self.sems[name] = dict(p=producer, c=0, step=step)

    def mark(self, name):
        return self.sems[name]["c"]

    def add(self, eng, fn, waits=(), inc=None, inc_n=None, fwaits=()):
        w = []
        for (s, v) in waits:
            if v is None or v <= 0:
                continue
            sem = self.sems[s]
            if sem["p"] == eng and self.prod_at[eng].get(s, 0) >= v:
                continue
            if self.waited[eng].get(s, 0) >= v:
                continue
            self.waited[eng][s] = v
            w.append((s, v))
        for (s, v) in fwaits:
            if v is None or v <= 0:
                continue
            if self.waited[eng].get(s, 0) >= v:
                continue
            self.waited[eng][s] = v
            w.append((s, v))
        incs = []
        if inc is not None:
            n = inc_n if inc_n is not None else self.sems[inc]["step"]
            self.sems[inc]["c"] += n
            incs.append((inc, n))
            if self.sems[inc]["p"] == eng:
                self.prod_at[eng][inc] = self.sems[inc]["c"]
        self.ops[eng].append((w, fn, incs))
        return self.mark(inc) if inc else None

    def emit(self, nc):
        import contextlib
        with contextlib.ExitStack() as st:
            handles = {}
            for name in self.sems:
                handles[name] = st.enter_context(nc.semaphore(name))
            for name in getattr(self, "extra_sems", []):
                handles[name] = st.enter_context(nc.semaphore(name))
            self.handles = handles
            block = st.enter_context(nc.Block())

            def run(eng_name, eng):
                for (w, fn, incs) in self.ops[eng_name]:
                    for (s, v) in w:
                        eng.wait_ge(handles[s], v)
                    inst = fn(eng)
                    for (s, n) in incs:
                        inst.then_inc(handles[s], n)

            @block.sync
            def _(sync):
                run("sync", sync)

            @block.gpsimd
            def _(gpsimd):
                run("gpsimd", gpsimd)

            @block.vector
            def _(vector):
                run("vector", vector)

            @block.scalar
            def _(scalar):
                run("scalar", scalar)

            @block.tensor
            def _(tensor):
                run("tensor", tensor)


class Banks:

    def __init__(self, prog):
        self.p = prog
        self.state = {}

    def reg(self, name, tensor):
        self.state[name] = dict(t=tensor, readers=[], writer=None)

    def t(self, name):
        return self.state[name]["t"]

    def write_waits(self, name):
        st = self.state[name]
        w = list(st["readers"])
        st["readers"] = []
        return w

    def wrote(self, name, sem, val):
        self.state[name]["writer"] = (sem, val)

    def read_waits(self, name):
        st = self.state[name]
        return [st["writer"]] if st["writer"] else []

    def read(self, name, sem, val):
        self.state[name]["readers"].append((sem, val))


def build(n_chunks=16, debug=False):
    nc = bass.Bass("TRN2", target_bir_lowering=False)
    TOK = n_chunks * TC

    x_d = nc.dram_tensor("x", [TOK + 3, D], F32, kind="ExternalInput")
    wrT_d = nc.dram_tensor("wrT", [D, D], F32, kind="ExternalInput")
    wiT_d = nc.dram_tensor("wiT", [D, D], F32, kind="ExternalInput")
    woT_d = nc.dram_tensor("woT", [D, D], F32, kind="ExternalInput")
    wfT_d = nc.dram_tensor("wfT", [D, D], F32, kind="ExternalInput")
    convd_d = nc.dram_tensor("convd", [32, 128, 128], F32, kind="ExternalInput")
    br_d = nc.dram_tensor("br_fm", [128, NF], F32, kind="ExternalInput")
    bi_d = nc.dram_tensor("bi_fm", [128, NF], F32, kind="ExternalInput")
    cb_d = nc.dram_tensor("cb_fm", [128, NF], F32, kind="ExternalInput")
    la_d = nc.dram_tensor("la_fm", [128, NF], F32, kind="ExternalInput")
    gam_d = nc.dram_tensor("gam_fm", [128, NF], F32, kind="ExternalInput")
    consts_d = nc.dram_tensor("consts", [128, 8], F32, kind="ExternalInput")
    eye_d = nc.dram_tensor("eye", [128, 128], F32, kind="ExternalInput")
    eyebf_d = nc.dram_tensor("eye_bf", [128, 128], BF16, kind="ExternalInput")
    ones_d = nc.dram_tensor("ones_bf", [128, 1], BF16, kind="ExternalInput")
    rmask_d = nc.dram_tensor("rmask", [128, NF], F32, kind="ExternalInput")
    out_d = nc.dram_tensor("out", [TOK, D], F32, kind="ExternalOutput")
    h_scr = nc.dram_tensor("h_scr", [n_chunks, 128, NF * TC], BF16)
    a_scr = nc.dram_tensor("a_scr", [n_chunks, 128, NF * TC], BF16)
    g_scr = nc.dram_tensor("g_scr", [n_chunks, 128, NF * TC], BF16)
    sh_x = nc.dram_tensor("sh_x", [2, 128, NF], F32, addr_space="Shared")
    sh_f = nc.dram_tensor("sh_f", [2, 16], mybir.dt.uint32, addr_space="Shared")
    slot_d = nc.dram_tensor("slot", [1, 4], mybir.dt.uint32, kind="ExternalInput")
    nonce_d = nc.dram_tensor("nonce", [1, 1], mybir.dt.uint32, kind="ExternalInput")

    A = nc.alloc_sbuf_tensor
    wr_bf = A("wr_bf", [128, NF * D], BF16)
    wi_bf = A("wi_bf", [128, NF * D], BF16)
    wo_bf = A("wo_bf", [128, NF * D], BF16)
    wf_bf = A("wf_bf", [128, NF * D], BF16)
    conv_bf = A("conv_bf", [128, 32 * 128], BF16)

    x_ln = A("x_ln", [128, 2, D], F32)
    xn_tok = A("xn_tok", [128, NTT * D], BF16)
    xn_feat = A("xn_feat", [128, NF, 515], BF16)
    xcb = A("xcb", [128, NF * TC], BF16)
    b_t = A("b_t", [128, NF * TC], F32)
    a_t = A("a_t", [128, NF * TC], F32)
    h_t = A("h_t", [128, NF * TC], F32)
    sct = A("sct", [128, NF * TC], BF16)
    A_t = A("A_t", [128, NF * TC], F32)
    g_t = A("g_t", [128, NF * TC], BF16)
    zeros_t = A("zeros_t", [128, TC], F32)
    gat = A("gat", [128, NF], F32)
    rmask = A("rmask_s", [128, NF], F32)
    slot_s = A("slot_s", [1, 4], mybir.dt.uint32)
    nonce_s = A("nonce_s", [1, 1], mybir.dt.uint32)
    flag_s = A("flag_s", [1, 1], mybir.dt.uint32)
    hent = A("hent", [128, NF], F32)
    acar = A("acar", [128, NF], F32)
    sq_rot = A("sq_rot", [128, 2, 256], BF16)
    rt_rot = A("rt_rot", [128, 512], F32)
    it_rot = A("it_rot", [128, 2, 512], BF16)
    tt_rot = A("tt_rot", [128, 2, 512], BF16)
    ff_rot = A("ff_rot", [128, 2, 512], BF16)
    carry = A("carry", [128, NF], F32)
    spc = A("spc", [128, 1], F32)

    st6 = A("st6", [128, 2, 12], F32)
    mv = A("mv", [128, 4, 2], F32)
    sd_t = A("sd_t", [128, 4], F32)
    rs_t = A("rs_t", [128, 4], F32)
    nb_t = A("nb_t", [128, 4], F32)
    rms_row = A("rms_row", [1, 256], F32)
    rms_tok = A("rms_tok", [128, 4], F32)

    br2 = A("br2", [128, NF], F32)
    bi2 = A("bi2", [128, NF], F32)
    cb_fm = A("cb_fm_s", [128, NF], F32)
    c4 = A("c4", [128, NF], F32)
    hgam = A("hgam", [128, NF], F32)
    la_fm = A("la_fm_s", [128, NF], F32)
    K = A("K_s", [128, 8], F32)
    eye = A("eye_s", [128, 128], F32)
    eye_bf = A("eye_bf_s", [128, 128], BF16)
    ones_bf = A("ones_bf_s", [128, 1], BF16)
    colsums = A("colsums", [128, NF], F32)
    allred = A("allred", [128, NF], F32)
    qsc = A("qsc", [128, 2], F32)

    def wfs_slot(q):
        return sct[:, (q % 2) * 2048:(q % 2) * 2048 + 2048].bitcast(F32)
    C_EPS5, C_HALF, C_QRT, C_NQRT, C_EPS6, C_I1024, C_NEG1 = range(7)

    def KC(i):
        return K[:, i:i + 1]

    pb = {}
    for nm in ("pA1", "pA2", "pB", "pC", "pD", "pE", "pF", "pG"):
        pb[nm] = nc.alloc_psum_tensor(nm, [128, 512], F32)

    P = Prog()
    P.extra_sems = ["pollq", "exdq"]
    BK = Banks(P)
    for nm, t in pb.items():
        BK.reg(nm, t)

    for nm, prod, step in [
        ("dX", "sync", 16), ("dXR", "sync", 16), ("dOUT", "sync", 16),
        ("dI", "sync", 16), ("dW", "gpsimd", 16), ("dF", "sync", 16),
        ("Vmv", "vector", 1), ("Ssd", "scalar", 1), ("Vln", "vector", 1),
        ("TxnT", "tensor", 1), ("VxnE", "vector", 1),
        ("Tconv", "tensor", 1), ("Vxcb", "vector", 1),
        ("Tr", "tensor", 1), ("SaE", "scalar", 1),
        ("Ti", "tensor", 1), ("SiE", "scalar", 1), ("Vb1", "vector", 1),
        ("Vam", "vector", 1), ("Ssct", "scalar", 1),
        ("Vscan", "vector", 1), ("Vcar", "vector", 1), ("Vhbf", "vector", 1),
        ("Tffn", "tensor", 1), ("SfE", "scalar", 1), ("Vg", "vector", 1),
        ("Tw", "tensor", 1), ("SwoE", "scalar", 1), ("Ssq", "scalar", 1),
        ("Tones", "tensor", 1), ("Srms", "scalar", 1), ("Vrecip", "vector", 1),
        ("TrmsT", "tensor", 1), ("Vrtok", "vector", 1), ("Vt2", "vector", 1),
        ("TsgT", "tensor", 1), ("Vfin", "vector", 1),
        ("Vq", "vector", 1), ("Gq", "gpsimd", 1), ("Tq", "tensor", 1),
        ("Sln", "scalar", 1), ("dSH", "gpsimd", 16), ("dSR", "sync", 16),
        ("dEX", "gpsimd", 16), ("Gcc", "gpsimd", 1), ("Vhe", "vector", 1),
        ("Vht", "vector", 1), ("VAc", "vector", 1),
        ("Sinit", "scalar", 1),
        ("Vinit", "vector", 1),
    ]:
        P.sem(nm, prod, step)

    init_loads = [
        (K, consts_d), (eye, eye_d), (eye_bf, eyebf_d), (ones_bf, ones_d),
        (br2, br_d), (bi2, bi_d), (cb_fm, cb_d), (la_fm, la_d), (hgam, gam_d),
        (rmask, rmask_d), (slot_s, slot_d), (nonce_s, nonce_d),
    ]
    for sb, dr in init_loads:
        P.add("sync", lambda e, sb=sb, dr=dr: e.dma_start(out=sb[:, :], in_=dr[:, :]),
              inc="dI")
    m_dI_small = P.mark("dI")

    def wload(sb, dr):
        return P.add("gpsimd",
                     lambda e, sb=sb, dr=dr: e.dma_start(
                         out=bass.AP(sb, 0, [[NF * D, 128], [D, NF], [1, D]]),
                         in_=bass.AP(dr, 0, [[D, 128], [128 * D, NF], [1, D]])),
                     inc="dW")
    m_wr = wload(wr_bf, wrT_d)
    m_wi = wload(wi_bf, wiT_d)
    m_wo = wload(wo_bf, woT_d)
    m_conv = P.add("gpsimd",
                   lambda e: e.dma_start(
                       out=bass.AP(conv_bf, 0, [[32 * 128, 128], [128, 32], [1, 128]]),
                       in_=bass.AP(convd_d, 0, [[128, 128], [128 * 128, 32], [1, 128]])),
                   inc="dW")

    P.add("scalar",
          lambda e: e.activation(c4[:, :], la_fm[:, :], AF.Sigmoid),
          waits=[("dI", m_dI_small)], inc="Sinit")
    P.add("scalar",
          lambda e: e.activation(c4[:, :], c4[:, :], AF.Ln),
          inc="Sinit")
    m_sp = P.mark("Sinit")
    P.add("vector", lambda e: e.tensor_scalar_mul(c4[:, :], c4[:, :], 4.0),
          waits=[("Sinit", m_sp)], inc="Vinit")
    P.add("vector", lambda e: e.tensor_scalar_mul(br2[:, :], br2[:, :], 0.5),
          waits=[("dI", m_dI_small)], inc="Vinit")
    P.add("vector", lambda e: e.tensor_scalar_mul(bi2[:, :], bi2[:, :], 0.5),
          inc="Vinit")
    P.add("vector", lambda e: e.tensor_scalar_mul(hgam[:, :], hgam[:, :], 0.5),
          inc="Vinit")
    P.add("vector", lambda e: e.memset(zeros_t[:, :], 0.0), inc="Vinit")
    P.add("vector", lambda e: e.memset(gat[:, :], 0.0), inc="Vinit")
    m_vinit = P.mark("Vinit")

    mF = []
    for q in range(NF):
        mF.append(P.add("sync",
                        lambda e, q=q: e.dma_start(out=wfs_slot(q),
                                                   in_=wfT_d[q * 128:(q + 1) * 128, :]),
                        waits=[("Vq", q - 1)],
                        inc="dF"))
        P.add("vector",
              lambda e, q=q: e.tensor_reduce(colsums[:, q:q + 1], wfs_slot(q),
                                             mybir.AxisListType.X, ALU.add,
                                             apply_absolute_value=True),
              waits=[("dF", 16 * (q + 1))], inc="Vq")
    m_abs = P.mark("Vq")
    P.add("vector", lambda e: e.tensor_reduce(allred[:, 0:1], colsums[:, :],
                                              mybir.AxisListType.X, ALU.add),
          fwaits=[("Vq", m_abs)], inc="Vq")
    ww = BK.write_waits("pD")
    P.add("tensor",
          lambda e: e.transpose(pb["pD"][0:1, 0:128], allred[:, 0:1], eye[:, :]),
          waits=ww + [("Vq", P.mark("Vq")), ("dI", m_dI_small)], inc="Tq")
    BK.wrote("pD", "Tq", P.mark("Tq"))
    P.add("vector",
          lambda e: e.tensor_copy(rms_row[0:1, 0:128], pb["pD"][0:1, 0:128]),
          waits=BK.read_waits("pD"), inc="Vq")
    BK.read("pD", "Vq", P.mark("Vq"))
    P.add("vector", lambda e: e.tensor_reduce(st6[0:1, 0, 0:1], rms_row[0:1, 0:128],
                                              mybir.AxisListType.X, ALU.add),
          fwaits=[("Vq", P.mark("Vq"))], inc="Vq")
    P.add("vector", lambda e: e.tensor_copy(
        rms_row[0:1, 128:256], st6[0:1, 0, 0:1].to_broadcast((1, 128))),
          fwaits=[("Vq", P.mark("Vq"))], inc="Vq")
    ww = BK.write_waits("pD")
    P.add("tensor",
          lambda e: e.transpose(pb["pD"][:, 256:257], rms_row[0:1, 128:256],
                                eye[0:1, 0:1]),
          waits=ww + [("Vq", P.mark("Vq"))], inc="Tq")
    BK.wrote("pD", "Tq", P.mark("Tq"))
    P.add("vector", lambda e: e.tensor_copy(allred[:, 1:2], pb["pD"][:, 256:257]),
          waits=BK.read_waits("pD"), inc="Vq")
    BK.read("pD", "Vq", P.mark("Vq"))
    P.add("vector", lambda e: e.tensor_scalar_mul(qsc[:, 0:1], allred[:, 1:2],
                                                  1.0 / (D * D)),
          fwaits=[("Vq", P.mark("Vq"))], inc="Vq")
    P.add("vector", lambda e: e.tensor_scalar_max(qsc[:, 0:1], qsc[:, 0:1], 1e-5),
          fwaits=[("Vq", P.mark("Vq"))], inc="Vq")
    P.add("vector", lambda e: e.reciprocal(qsc[:, 1:2], qsc[:, 0:1]),
          fwaits=[("Vq", P.mark("Vq"))], inc="Vq")
    m_qsc = P.mark("Vq")
    qt, q1, q2 = a_t[:, 0:D], a_t[:, D:2 * D], a_t[:, 2 * D:3 * D]
    for q in range(NF):
        P.add("sync",
              lambda e, q=q: e.dma_start(out=wfs_slot(q),
                                         in_=wfT_d[q * 128:(q + 1) * 128, :]),
              waits=[("Vq", m_qsc if q < 2 else m_qsc + 5 * (q - 1))],
              inc="dF")
        P.add("vector",
              lambda e, q=q: e.tensor_scalar_mul(qt, wfs_slot(q), qsc[:, 1:2]),
              waits=[("dF", 16 * (NF + q + 1))],
              fwaits=[("Vq", m_qsc)], inc="Vq")
        P.add("vector",
              lambda e: e.tensor_single_scalar(q1, qt, 0.5, ALU.is_ge), inc="Vq")
        P.add("vector",
              lambda e: e.tensor_single_scalar(q2, qt, -0.5, ALU.is_le), inc="Vq")
        P.add("vector", lambda e: e.tensor_sub(q1, q1, q2), inc="Vq")
        P.add("vector",
              lambda e, q=q: e.tensor_scalar_mul(wf_bf[:, q * D:(q + 1) * D], q1,
                                                 qsc[:, 0:1]),
              inc="Vq")
    m_quant = P.mark("Vq")

    m_ln = {}
    m_fin = {}
    m_dout = {}
    m_dxr = {}
    m_scanC = {}
    m_bmul = {}
    m_car = {}
    m_acar = {}
    m_xnE = {}
    m_xcb = {}
    m_xcball = {}
    m_aAll = {}
    m_b1All = {}
    m_convAll = {}
    m_store = {}
    m_gstore = {}
    m_read = {}
    m_ht = {}

    rotbank_fr = ["pD", "pE"]
    fr_idx = [0]

    def next_fr():
        b = rotbank_fr[fr_idx[0] % 2]
        fr_idx[0] += 1
        return b

    def OH(c0, c1):
        return sct[:, 2 * c0:2 * c1].bitcast(F32)

    def h_slot(p):
        return a_t[:, p * 2048:(p + 1) * 2048].bitcast(BF16)

    def A_slot(p):
        return b_t[:, p * 2048:(p + 1) * 2048].bitcast(BF16)

    def g_slot(p, lo=0, hi=NF * TC):
        return h_t[:, p * 2048 + lo // 2: p * 2048 + hi // 2].bitcast(BF16)

    P.add("sync", lambda e: e.dma_start(out=x_ln[0:3, 1, :], in_=x_d[0:3, :]),
          inc="dI")
    m_dlead = P.mark("dI")
    P.add("vector", lambda e: e.bn_stats(st6[0:3, 1, 0:6], x_ln[0:3, 1, 0:512]),
          waits=[("dI", m_dlead)], inc="Vmv")
    P.add("vector", lambda e: e.bn_stats(st6[0:3, 1, 6:12], x_ln[0:3, 1, 512:1024]),
          inc="Vmv")
    m_lst = P.mark("Vmv")
    P.add("vector", lambda e: e.bn_aggr(mv[0:3, 0, :], st6[0:3, 1, 0:12]),
          fwaits=[("Vmv", m_lst)], inc="Vmv")
    m_lag = P.mark("Vmv")
    P.add("scalar", lambda e: e.activation(sd_t[0:3, 0:1], mv[0:3, 0, 1:2],
                                           AF.Sqrt, bias=K[0:3, C_EPS5:C_EPS5 + 1]),
          waits=[("Vmv", m_lag), ("dI", m_dI_small)], inc="Ssd")
    m_lsd = P.mark("Ssd")
    P.add("vector", lambda e: e.reciprocal(rs_t[0:3, 0:1], sd_t[0:3, 0:1]),
          waits=[("Ssd", m_lsd)], inc="Vln")
    m_lrs = P.mark("Vln")
    P.add("vector", lambda e: e.scalar_tensor_tensor(nb_t[0:3, 0:1], mv[0:3, 0, 0:1],
                                                     -1.0, rs_t[0:3, 0:1],
                                                     ALU.mult, ALU.mult),
          fwaits=[("Vln", m_lrs)], inc="Vln")
    m_lnb = P.mark("Vln")
    P.add("scalar", lambda e: e.activation(g_t[0:3, 0:1024], x_ln[0:3, 1, :],
                                           AF.Identity, bias=nb_t[0:3, 0:1],
                                           scale=rs_t[0:3, 0:1]),
          waits=[("Vln", m_lnb)], inc="Sln")
    m_lead = P.mark("Sln")
    for j in range(NF):
        bank = "pA1" if j % 2 == 0 else "pA2"
        ww = BK.write_waits(bank)
        P.add("tensor",
              lambda e, j=j, bank=bank: e.transpose(
                  BK.t(bank)[:, 0:2].bitcast(BF16)[:, 0:3],
                  g_t[0:3, j * 128:(j + 1) * 128],
                  eye_bf[0:3, 0:3]),
              waits=ww + [("Sln", m_lead), ("dI", m_dI_small)], inc="TxnT")
        BK.wrote(bank, "TxnT", P.mark("TxnT"))
        P.add("vector",
              lambda e, j=j, bank=bank: e.tensor_copy(
                  xn_feat[:, j, 0:3], BK.t(bank)[:, 0:2].bitcast(BF16)[:, 0:3]),
              waits=BK.read_waits(bank), inc="VxnE")
        BK.read(bank, "VxnE", P.mark("VxnE"))

    def emit_lnload(k):
        for t in range(NTT):
            g = 4 * k + t
            xw = [("Sln", m_ln.get(g - 2, 0))]
            if g == 1:
                xw.append(("Sln", m_lead))
            P.add("sync",
                  lambda e, g=g: e.dma_start(
                      out=x_ln[:, g % 2, :],
                      in_=x_d[3 + g * 128:3 + (g + 1) * 128, :]),
                  waits=xw, inc="dX")
            P.add("vector",
                  lambda e, t=t, g=g: e.bn_stats(st6[:, t % 2, 0:6],
                                                 x_ln[:, g % 2, 0:512]),
                  waits=[("dX", 16 * (g + 1))], inc="Vmv")
            P.add("vector",
                  lambda e, t=t, g=g: e.bn_stats(st6[:, t % 2, 6:12],
                                                 x_ln[:, g % 2, 512:1024]),
                  inc="Vmv")
            m_st = P.mark("Vmv")
            P.add("vector",
                  lambda e, t=t: e.bn_aggr(mv[:, t, :], st6[:, t % 2, 0:12]),
                  fwaits=[("Vmv", m_st)], inc="Vmv")
            m_ag = P.mark("Vmv")
            P.add("scalar",
                  lambda e, t=t: e.activation(sd_t[:, t:t + 1], mv[:, t, 1:2],
                                              AF.Sqrt, bias=KC(C_EPS5)),
                  waits=[("Vmv", m_ag)], inc="Ssd")
            m_sd = P.mark("Ssd")
            P.add("vector",
                  lambda e, t=t: e.reciprocal(rs_t[:, t:t + 1], sd_t[:, t:t + 1]),
                  waits=[("Ssd", m_sd)], inc="Vln")
            m_rs = P.mark("Vln")
            P.add("vector",
                  lambda e, t=t: e.scalar_tensor_tensor(nb_t[:, t:t + 1],
                                                        mv[:, t, 0:1], -1.0,
                                                        rs_t[:, t:t + 1],
                                                        ALU.mult, ALU.mult),
                  fwaits=[("Vln", m_rs)], inc="Vln")
            m_nb = P.mark("Vln")
            m_ln[g] = P.add("scalar",
                            lambda e, g=g, t=t: e.activation(
                                xn_tok[:, t * D:(t + 1) * D], x_ln[:, g % 2, :],
                                AF.Identity, bias=nb_t[:, t:t + 1],
                                scale=rs_t[:, t:t + 1]),
                            waits=[("TxnT", P.mark("TxnT")), ("Vln", m_nb)],
                            inc="Sln")

    def emit_front_head(k):
        if k > 0:
            P.add("vector",
                  lambda e: e.tensor_copy(xn_feat[:, :, 0:3], xn_feat[:, :, 512:515]),
                  waits=[("Tconv", m_convAll[k - 1])], inc="Vln")

        for j in range(NF):
            bank = "pA1" if j % 2 == 0 else "pA2"
            ww = BK.write_waits(bank)
            for t in range(NTT):
                P.add("tensor",
                      lambda e, j=j, t=t, bank=bank: e.transpose(
                          BK.t(bank)[:, 0:256].bitcast(BF16)[:, t * 128:(t + 1) * 128],
                          xn_tok[:, t * D + j * 128: t * D + (j + 1) * 128],
                          eye_bf[:, :]),
                      waits=(ww if t == 0 else []) + [("Sln", m_ln[4 * k + NTT - 1])],
                      inc="TxnT" if t == NTT - 1 else None)
            BK.wrote(bank, "TxnT", P.mark("TxnT"))
            m_xnE[(k, j)] = P.add(
                "vector",
                lambda e, j=j, bank=bank: e.tensor_copy(
                    xn_feat[:, j, 3:515], BK.t(bank)[:, 0:256].bitcast(BF16)),
                waits=BK.read_waits(bank) + [("Vinit", m_vinit),
                                             ("Tffn", P.mark("Tffn"))],
                inc="VxnE")
            BK.read(bank, "VxnE", m_xnE[(k, j)])

        for j in range(NF):
            bank = "pB" if j % 2 == 0 else "pC"
            ww = BK.write_waits(bank)
            for tap in range(4):
                P.add("tensor",
                      lambda e, j=j, tap=tap, bank=bank: e.matmul(
                          BK.t(bank)[:, 0:512],
                          conv_bf[:, (j * 4 + tap) * 128:(j * 4 + tap + 1) * 128],
                          xn_feat[:, j, tap:tap + 512],
                          start=(tap == 0), stop=(tap == 3)),
                      waits=(ww if tap == 0 else []) + [("VxnE", m_xnE[(k, j)]),
                                                        ("dW", m_conv)],
                      inc="Tconv" if tap == 3 else None)
            BK.wrote(bank, "Tconv", P.mark("Tconv"))
            if j % 2 == 0:
                m_xcb[(k, j)] = P.add(
                    "scalar",
                    lambda e, j=j, bank=bank: e.activation(
                        xcb[:, j * TC:(j + 1) * TC], BK.t(bank)[:, 0:512],
                        AF.Identity, bias=cb_fm[:, j:j + 1]),
                    waits=BK.read_waits(bank) + [("Vb1", P.mark("Vb1"))],
                    inc="Vxcb", )
                BK.read(bank, "Vxcb", m_xcb[(k, j)])
            else:
                m_xcb[(k, j)] = P.add(
                    "vector",
                    lambda e, j=j, bank=bank: e.tensor_scalar(
                        xcb[:, j * TC:(j + 1) * TC], BK.t(bank)[:, 0:512],
                        cb_fm[:, j:j + 1], None, ALU.add),
                    waits=BK.read_waits(bank) + [("Ti", P.mark("Ti")),
                                                 ("Vb1", P.mark("Vb1"))], inc="Vxcb")
                BK.read(bank, "Vxcb", m_xcb[(k, j)])
        m_convAll[k] = P.mark("Tconv")
        m_xcball[k] = P.mark("Vxcb")


    def emit_front_mid(k):
        for j in range(NF):
            bank = next_fr()
            ww = BK.write_waits(bank)
            for q in range(NF):
                P.add("tensor",
                      lambda e, j=j, q=q, bank=bank: e.matmul(
                          BK.t(bank)[:, 0:512],
                          wr_bf[:, q * D + j * 128: q * D + (j + 1) * 128],
                          xcb[:, q * TC:(q + 1) * TC],
                          start=(q == 0), stop=(q == NF - 1)),
                      waits=(ww if q == 0 else []) + [("Vxcb", m_xcball[k]), ("dW", m_wr)],
                      inc="Tr" if q == NF - 1 else None)
            BK.wrote(bank, "Tr", P.mark("Tr"))
            P.add("scalar",
                  lambda e, j=j, bank=bank: e.activation(
                      rt_rot[:, :], BK.t(bank)[:, 0:512], AF.Tanh,
                      bias=br2[:, j:j + 1], scale=KC(C_HALF)),
                  waits=BK.read_waits(bank) + [("Vinit", m_vinit)], inc="SaE")
            BK.read(bank, "SaE", P.mark("SaE"))
            P.add("scalar",
                  lambda e, j=j: e.activation(
                      a_t[:, j * TC:(j + 1) * TC], rt_rot[:, :], AF.Exp,
                      bias=c4[:, j:j + 1], scale=c4[:, j:j + 1]),
                  waits=[("VAc", m_acar.get(k - 1, 0)),
                         ("Vq", m_quant if k == 0 else 0)],
                  inc="SaE")
        m_aAll[k] = P.mark("SaE")

        for j in range(NF):
            bank = next_fr()
            ww = BK.write_waits(bank)
            for q in range(NF):
                P.add("tensor",
                      lambda e, j=j, q=q, bank=bank: e.matmul(
                          BK.t(bank)[:, 0:512],
                          wi_bf[:, q * D + j * 128: q * D + (j + 1) * 128],
                          xcb[:, q * TC:(q + 1) * TC],
                          start=(q == 0), stop=(q == NF - 1)),
                      waits=(ww if q == 0 else []) + [("dW", m_wi)],
                      inc="Ti" if q == NF - 1 else None)
            BK.wrote(bank, "Ti", P.mark("Ti"))
            P.add("scalar",
                  lambda e, j=j, bank=bank: e.activation(
                      it_rot[:, j % 2, :], BK.t(bank)[:, 0:512], AF.Tanh,
                      bias=bi2[:, j:j + 1], scale=KC(C_HALF)),
                  waits=BK.read_waits(bank) + [("Vb1", max(0, P.mark("Vb1") - 1))],
                  inc="SiE")
            BK.read(bank, "SiE", P.mark("SiE"))
            m_sie = P.mark("SiE")
            P.add("vector",
                  lambda e, j=j: e.scalar_tensor_tensor(
                      b_t[:, j * TC:(j + 1) * TC], it_rot[:, j % 2, :], 1.0,
                      xcb[:, j * TC:(j + 1) * TC], ALU.add, ALU.mult),
                  waits=[("SiE", m_sie)], inc="Vb1")
        m_b1All[k] = P.mark("Vb1")

        for j in range(NF):
            bank = next_fr()
            ww = BK.write_waits(bank)
            for q in range(NF):
                P.add("tensor",
                      lambda e, j=j, q=q, bank=bank: e.matmul(
                          BK.t(bank)[:, 0:512],
                          wf_bf[:, q * D + j * 128: q * D + (j + 1) * 128],
                          xn_feat[:, q, 3:515],
                          start=(q == 0), stop=(q == NF - 1)),
                      waits=(ww if q == 0 else []) + [("Vq", m_quant),
                                                      ("VxnE", m_xnE[(k, NF - 1)])],
                      inc="Tffn" if q == NF - 1 else None)
            BK.wrote(bank, "Tffn", P.mark("Tffn"))
            P.add("scalar",
                  lambda e, j=j, bank=bank: e.activation(
                      tt_rot[:, j % 2, :], BK.t(bank)[:, 0:512], AF.Tanh,
                      scale=KC(C_HALF)),
                  waits=BK.read_waits(bank) + [("Vg", max(0, P.mark("Vg") - 1))],
                  inc="SfE")
            P.add("scalar",
                  lambda e, j=j, bank=bank: e.activation(
                      ff_rot[:, j % 2, :], BK.t(bank)[:, 0:512], AF.Identity,
                      scale=hgam[:, j:j + 1]),
                  inc="SfE")
            BK.read(bank, "SfE", P.mark("SfE"))
            m_fe = P.mark("SfE")
            P.add("vector",
                  lambda e, j=j: e.scalar_tensor_tensor(
                      g_t[:, j * TC:(j + 1) * TC], tt_rot[:, j % 2, :], 1.0,
                      ff_rot[:, j % 2, :], ALU.add, ALU.mult),
                  waits=[("SfE", m_fe), ("dSH", m_gstore.get(k - 1, 0))], inc="Vg")
        mg = P.mark("Vg")
        P.add("gpsimd", lambda e, k=k: e.dma_start(out=g_scr[k, :, :], in_=g_t[:, :]),
              waits=[("Vg", mg)], inc="dSH")
        m_gstore[k] = P.mark("dSH")


    def emit_front_tail(k):
        P.add("vector",
              lambda e: e.tensor_scalar_min(a_t[:, :], a_t[:, :], 1.0),
              waits=[("SaE", m_aAll[k])], inc="Vam")
        m_am = P.mark("Vam")
        P.add("scalar",
              lambda e: e.activation(h_t[:, :], a_t[:, :], AF.Square),
              waits=[("Vam", m_am), ("dSH", m_store.get(k - 1, (0, 0))[1]),
                     ("Vcar", m_car.get(k - 1, 0))],
              inc="Ssct")
        P.add("scalar",
              lambda e: e.activation(sct[:, :], h_t[:, :], AF.Sqrt,
                                     bias=KC(C_QRT), scale=KC(C_NQRT)),
              waits=[("Vb1", m_bmul.get(k - 1, 0))],
              inc="Ssct")
        m_sct = P.mark("Ssct")
        P.add("vector",
              lambda e: e.tensor_mul(b_t[:, :], b_t[:, :], sct[:, :]),
              waits=[("Ssct", m_sct), ("Vb1", m_b1All[k])], inc="Vb1")
        m_bmul[k] = P.mark("Vb1")
        for j in range(NF):
            init_h = 0.0 if k == 0 else carry[:, j:j + 1]
            P.add("vector",
                  lambda e, j=j, init_h=init_h: e.tensor_tensor_scan(
                      h_t[:, j * TC:(j + 1) * TC], a_t[:, j * TC:(j + 1) * TC],
                      b_t[:, j * TC:(j + 1) * TC], init_h, ALU.mult, ALU.add),
                  waits=[("Ssct", m_sct)], inc="Vscan")
        m_scanC[k] = P.mark("Vscan")
        m_car[k] = P.add("vector",
                         lambda e: e.tensor_copy(
                             bass.AP(carry, 0, [[NF, 128], [1, NF], [1, 1]]),
                             bass.AP(h_t, TC - 1, [[NF * TC, 128], [TC, NF], [1, 1]])),
                         fwaits=[("Vscan", m_scanC[k])], inc="Vcar")
        for j in range(NF):
            init_a = 1.0 if k == 0 else acar[:, j:j + 1]
            P.add("vector",
                  lambda e, j=j, init_a=init_a: e.tensor_tensor_scan(
                      A_t[:, j * TC:(j + 1) * TC], a_t[:, j * TC:(j + 1) * TC],
                      zeros_t[:, :], init_a, ALU.mult, ALU.add),
                  waits=([("dSH", m_store.get(k - 1, (0, 0))[1])] if j == 0 else []),
                  inc="Vscan")
        m_acar[k] = P.add("vector",
                          lambda e: e.tensor_copy(
                              bass.AP(acar, 0, [[NF, 128], [1, NF], [1, 1]]),
                              bass.AP(A_t, TC - 1, [[NF * TC, 128], [TC, NF], [1, 1]])),
                          fwaits=[("Vscan", P.mark("Vscan"))], inc="VAc")
        P.add("gpsimd", lambda e, k=k: e.dma_start(out=h_scr[k, :, :], in_=h_t[:, :]),
              waits=[("Vcar", m_car[k])], inc="dSH")
        P.add("gpsimd", lambda e, k=k: e.dma_start(out=a_scr[k, :, :], in_=A_t[:, :]),
              waits=[("VAc", m_acar[k])], inc="dSH")
        m_store[k] = (P.mark("Vhbf"), P.mark("dSH"))


    def emit_exchange():
        def ex_fn(e):
            rs_r = e.alloc_register("exslot")
            rp_r = e.alloc_register("expslot")
            rn = e.alloc_register("exnonce")
            rf = e.alloc_register("exflag")
            rv = e.alloc_register("expollv")
            e.reg_load(rs_r, slot_s[0:1, 0:1])
            e.reg_load(rp_r, slot_s[0:1, 1:2])
            e.reg_load(rn, nonce_s[0:1, 0:1])
            rs = e.snap(rs_r, min_val=0, max_val=1)
            rp = e.snap(rp_r, min_val=0, max_val=1)
            pq = P.handles["pollq"]
            dq = P.handles["exdq"]
            e.dma_start(out=sh_x[bass.ds(rs, 1), :, :],
                        in_=carry[:, :]).then_inc(dq, 16)
            e.wait_ge(dq, 16)
            e.dma_start(out=sh_f[bass.ds(rs, 1), 0:1],
                        in_=nonce_s[0:1, :]).then_inc(dq, 16)
            e.wait_ge(dq, 32)
            rneed = e.alloc_register("exneed")
            e.reg_load(rneed, slot_s[0:1, 2:3])
            e.reg_mul(rn, rn, rneed)
            e.reg_mov(rf, 0)
            e.reg_mov(rv, 0)
            with e.While(lambda: e.snap(rf, donate=False)
                         != e.snap(rn, donate=False)):
                e.reg_add(rv, rv, 16)
                e.dma_start(out=flag_s[0:1, :],
                            in_=sh_f[bass.ds(rp, 1), 0:1]).then_inc(pq, 16)
                e.wait_ge(pq, rv)
                e.reg_load(rf, flag_s[0:1, 0:1])
            nv = e.snap(rneed, min_val=0, max_val=1)
            e.dma_start(out=gat[:, :],
                        in_=sh_x[bass.ds(rp, 1), :, :], cond=nv).then_inc(dq, 16)
            e.wait_ge(dq, 48)
            return e.nop()

        P.add("gpsimd", ex_fn,
              waits=[("Vcar", m_car[n_chunks - 1]), ("dI", m_dI_small)],
              inc="Gcc")
        P.add("vector", lambda e: e.tensor_mul(hent[:, :], gat[:, :], rmask[:, :]),
              waits=[("Gcc", 1)], inc="Vhe")

    m_vhe_done = [0]
    p1_final = []

    def emit_read(k):
        if k >= n_chunks:
            return
        p = k % 2
        w = [("dSH", m_store[k][1]), ("Vht", m_ht.get(k - 2, 0)),
             ("Tw", P.mark("Tw")), ("Vt2", P.mark("Vt2"))] + p1_final
        P.add("sync", lambda e, k=k, p=p: e.dma_start(out=h_slot(p), in_=h_scr[k, :, :]),
              waits=w, inc="dSR")
        P.add("sync", lambda e, k=k, p=p: e.dma_start(out=A_slot(p), in_=a_scr[k, :, :]),
              waits=[("dSH", m_store[k][1])] + p1_final, inc="dSR")
        P.add("sync", lambda e, k=k, p=p: e.dma_start(out=g_slot(p), in_=g_scr[k, :, :]),
              waits=[("dSH", m_gstore[k])] + p1_final, inc="dSR")
        m_read[k] = P.mark("dSR")

    def emit_back(k):
        p = k % 2
        for j in range(NF):
            P.add("vector",
                  lambda e, j=j, p=p: e.scalar_tensor_tensor(
                      xcb[:, j * TC:(j + 1) * TC],
                      A_slot(p)[:, j * TC:(j + 1) * TC],
                      hent[:, j:j + 1],
                      h_slot(p)[:, j * TC:(j + 1) * TC],
                      ALU.mult, ALU.add),
                  waits=[("dSR", m_read[k]),
                         ("Tw", P.mark("Tw"))] + p1_final,
                  fwaits=[("Vhe", m_vhe_done[0])],
                  inc="Vht")
        m_ht[k] = P.mark("Vht")

        for hh in range(2):
            for j in range(NF):
                bank = "pF" if j % 2 == 0 else "pG"
                ww = BK.write_waits(bank)
                for q in range(NF):
                    P.add("tensor",
                          lambda e, j=j, q=q, hh=hh, bank=bank: e.matmul(
                              BK.t(bank)[:, 0:256],
                              wo_bf[:, q * D + j * 128: q * D + (j + 1) * 128],
                              xcb[:, q * TC + hh * 256: q * TC + hh * 256 + 256],
                              start=(q == 0), stop=(q == NF - 1)),
                          waits=(ww if q == 0 else []) + [("Vht", m_ht[k]),
                                                          ("dW", m_wo)],
                          inc="Tw" if q == NF - 1 else None)
                BK.wrote(bank, "Tw", P.mark("Tw"))
                P.add("scalar",
                      lambda e, j=j, bank=bank: e.activation(
                          OH(j * 256, (j + 1) * 256), BK.t(bank)[:, 0:256],
                          AF.Identity),
                      waits=BK.read_waits(bank) + [("TsgT", P.mark("TsgT"))] + p1_final,
                      inc="SwoE")
                BK.read(bank, "SwoE", P.mark("SwoE"))
                P.add("scalar",
                      lambda e, j=j: e.activation(
                          sq_rot[:, j % 2, :], OH(j * 256, (j + 1) * 256),
                          AF.Square),
                      inc="Ssq")
                m_sq = P.mark("Ssq")
                wwB = BK.write_waits("pB") if j == 0 else []
                P.add("tensor",
                      lambda e, j=j, hh=hh: e.matmul(
                          pb["pB"][0:1, hh * 256:hh * 256 + 256],
                          ones_bf[:, 0:1], sq_rot[:, j % 2, :],
                          start=(j == 0), stop=(j == NF - 1)),
                      waits=wwB + [("Ssq", m_sq)],
                      inc="Tones" if j == NF - 1 else None)
            BK.wrote("pB", "Tones", P.mark("Tones"))
            P.add("scalar",
                  lambda e, hh=hh: e.activation(
                      rms_row[0:1, :], pb["pB"][0:1, hh * 256:hh * 256 + 256],
                      AF.Sqrt, bias=K[0:1, C_EPS6:C_EPS6 + 1],
                      scale=K[0:1, C_I1024:C_I1024 + 1]),
                  waits=BK.read_waits("pB") + [("Vrecip", P.mark("Vrecip")),
                                               ("TrmsT", P.mark("TrmsT"))],
                  inc="Srms")
            BK.read("pB", "Srms", P.mark("Srms"))
            m_srms = P.mark("Srms")
            P.add("vector", lambda e: e.reciprocal(rms_row[0:1, :], rms_row[0:1, :]),
                  waits=[("Srms", m_srms)], inc="Vrecip")
            m_rc = P.mark("Vrecip")
            for tt in range(2):
                tile = hh * 2 + tt
                ww = BK.write_waits("pC")
                P.add("tensor",
                      lambda e, tt=tt, tile=tile: e.transpose(
                          pb["pC"][:, 384 + tile:384 + tile + 1],
                          rms_row[0:1, tt * 128:(tt + 1) * 128],
                          eye[0:1, 0:1]),
                      waits=ww + [("Vrecip", m_rc)], inc="TrmsT")
                BK.wrote("pC", "TrmsT", P.mark("TrmsT"))
                m_tt = P.mark("TrmsT")
                P.add("vector",
                      lambda e, tile=tile: e.tensor_copy(
                          rms_tok[:, tile:tile + 1],
                          pb["pC"][:, 384 + tile:384 + tile + 1]),
                      waits=[("TrmsT", m_tt)], inc="Vrtok")
                BK.read("pC", "Vrtok", P.mark("Vrtok"))
            for j in range(NF):
                P.add("vector",
                      lambda e, j=j, hh=hh, p=p: e.tensor_tensor(
                          OH(j * 256, (j + 1) * 256),
                          OH(j * 256, (j + 1) * 256),
                          g_slot(p, j * TC + hh * 256, j * TC + hh * 256 + 256),
                          ALU.mult),
                      waits=[("SwoE", P.mark("SwoE")), ("dSR", m_read[k])],
                      inc="Vt2")
            m_t2 = P.mark("Vt2")
            for tt in range(2):
                tile = hh * 2 + tt
                g = 4 * k + tile
                pair = ("pA1", "pA2") if tt == 0 else ("pD", "pE")
                P.add("sync",
                      lambda e, g=g: e.dma_start(
                          out=x_ln[:, g % 2, :],
                          in_=x_d[3 + g * 128:3 + (g + 1) * 128, :]),
                      waits=[("dOUT", m_dout.get(g - 2, 0)),
                             ("Sln", m_ln[4 * n_chunks - 1])], inc="dXR")
                m_dxr[g] = P.mark("dXR")
                ww0 = BK.write_waits(pair[0])
                ww1 = BK.write_waits(pair[1])
                for j in range(NF):
                    bk = pair[0] if j < 4 else pair[1]
                    P.add("tensor",
                          lambda e, j=j, tt=tt, bk=bk: e.transpose(
                              BK.t(bk)[:, (j % 4) * 128:(j % 4) * 128 + 128],
                              OH(j * 256 + tt * 128, j * 256 + tt * 128 + 128),
                              eye[:, :]),
                          waits=(ww0 + ww1 if j == 0 else []) + [("Vt2", m_t2)],
                          inc="TsgT" if j == NF - 1 else None)
                BK.wrote(pair[0], "TsgT", P.mark("TsgT"))
                BK.wrote(pair[1], "TsgT", P.mark("TsgT"))
                m_sgt = P.mark("TsgT")
                for half_pb in range(2):
                    bk = pair[half_pb]
                    P.add("vector",
                          lambda e, g=g, tile=tile, bk=bk, half_pb=half_pb:
                          e.scalar_tensor_tensor(
                              x_ln[:, g % 2, half_pb * 512:(half_pb + 1) * 512],
                              BK.t(bk)[:, 0:512],
                              rms_tok[:, tile:tile + 1],
                              x_ln[:, g % 2, half_pb * 512:(half_pb + 1) * 512],
                              ALU.mult, ALU.add),
                          waits=[("TsgT", m_sgt), ("dXR", m_dxr[g])],
                          fwaits=[("Vrtok", P.mark("Vrtok"))],
                          inc="Vfin")
                    BK.read(bk, "Vfin", P.mark("Vfin"))
                m_fin[g] = P.mark("Vfin")
                m_dout[g] = P.add("sync",
                                  lambda e, g=g: e.dma_start(
                                      out=out_d[g * 128:(g + 1) * 128, :],
                                      in_=x_ln[:, g % 2, :]),
                                  waits=[("Vfin", m_fin[g])], inc="dOUT")

    emit_lnload(0)
    emit_front_head(0)
    for k in range(n_chunks):
        emit_front_mid(k)
        if k + 1 < n_chunks:
            emit_lnload(k + 1)
            emit_front_head(k + 1)
        emit_front_tail(k)
    p1_final.extend([("Vscan", P.mark("Vscan")), ("VAc", P.mark("VAc")),
                     ("Vcar", P.mark("Vcar")), ("dSH", P.mark("dSH")),
                     ("Vb1", P.mark("Vb1")), ("Tffn", P.mark("Tffn")),
                     ("Ssct", P.mark("Ssct"))])
    emit_exchange()
    m_vhe_done[0] = P.mark("Vhe")
    emit_read(0)
    emit_read(1)
    for k in range(n_chunks):
        emit_back(k)
        emit_read(k + 2)

    if debug:
        dbg = {}
        for nm, t in [("a", a_t), ("b", b_t), ("h", h_t), ("xcb", xcb),
                      ("xntok", xn_tok), ("mvd", mv),
                      ("carryd", carry), ("acard", acar), ("hentd", hent),
                      ("gatd", gat), ("At", A_t),
                      ("g", g_t), ("xnf", xn_feat), ("wfq", wf_bf),
                      ("c4d", c4), ("qscd", qsc), ("rmstokd", rms_tok)]:
            shp = list(t.shape)
            flat = [shp[0], int(np.prod(shp[1:]))]
            dt_ = t.dtype
            od = nc.dram_tensor("dbg_" + nm, flat, dt_, kind="ExternalOutput")
            dbg[nm] = (t, od, flat)
        wl = [(nm, 16 * P.sems[nm]["c"] if P.sems[nm]["step"] == 16 else P.sems[nm]["c"])
              for nm in P.sems]
        wl = [(nm, P.sems[nm]["c"]) for nm in P.sems]
        for nm, (t, od, flat) in dbg.items():
            P.add("sync",
                  lambda e, t=t, od=od, flat=flat: e.dma_start(
                      out=od[:, :],
                      in_=bass.AP(t, 0, [[flat[1], flat[0]], [1, flat[1]]])),
                  waits=[("Vfin", P.mark("Vfin")), ("dOUT", P.mark("dOUT"))],
                  inc="dOUT")

    P.add("sync", lambda e: e.nop(),
          waits=[("dOUT", P.mark("dOUT"))])

    P.emit(nc)
    return nc


_NC_CACHE = {}


def _get_nc(n_chunks):
    if n_chunks not in _NC_CACHE:
        _NC_CACHE[n_chunks] = build(n_chunks)
    return _NC_CACHE[n_chunks]


def make_inputs(x_full, conv_w, conv_b, W_r, b_r, W_i, b_i, log_a, W_out, gamma,
                W_ffn):
    def fm(v):
        return np.ascontiguousarray(v.reshape(NF, 128).T).astype(np.float32)

    convd = np.zeros((32, 128, 128), np.float32)
    cw = np.asarray(conv_w).reshape(D, 4)
    for j in range(NF):
        for kk in range(4):
            np.fill_diagonal(convd[j * 4 + kk], cw[j * 128:(j + 1) * 128, kk])

    consts = np.zeros((128, 8), np.float32)
    consts[:, 0] = 1e-5
    consts[:, 1] = 0.5
    consts[:, 2] = 0.25
    consts[:, 3] = -0.25
    consts[:, 4] = 1e-6
    consts[:, 5] = 1.0 / D
    consts[:, 6] = -1.0

    shared = dict(
        wrT=np.ascontiguousarray(np.asarray(W_r).T),
        wiT=np.ascontiguousarray(np.asarray(W_i).T),
        woT=np.ascontiguousarray(np.asarray(W_out).T),
        wfT=np.ascontiguousarray(np.asarray(W_ffn).T),
        convd=convd,
        br_fm=fm(np.asarray(b_r)), bi_fm=fm(np.asarray(b_i)),
        cb_fm=fm(np.asarray(conv_b)), la_fm=fm(np.asarray(log_a)),
        gam_fm=fm(np.asarray(gamma)),
        consts=consts,
        eye=np.eye(128, dtype=np.float32),
        eye_bf=np.eye(128, dtype=ml_dtypes.bfloat16),
        ones_bf=np.ones((128, 1), ml_dtypes.bfloat16),
    )
    x = np.asarray(x_full)
    B, L, d = x.shape
    half = L // 2
    nonce = np.array([[np.random.randint(1, 2**31)]], np.uint32)
    in_maps = []
    for c in range(8):
        b, h = c // 2, c % 2
        m = dict(shared)
        if h == 0:
            xh = np.concatenate([np.zeros((3, d), np.float32), x[b, 0:half]], axis=0)
        else:
            xh = x[b, half - 3:L]
        m["x"] = np.ascontiguousarray(xh)
        rm = np.zeros((128, NF), np.float32)
        if h == 1:
            rm[:, :] = 1.0
        m["rmask"] = rm
        m["slot"] = np.array([[h, 1 - h, h, 0]], np.uint32)
        m["nonce"] = nonce
        in_maps.append(m)
    return in_maps


def kernel(L0_attn_x, L0_attn_conv_w, L0_attn_conv_b, L0_attn_W_r, L0_attn_b_r,
           L0_attn_W_i, L0_attn_b_i, L0_attn_log_a, L0_attn_W_out, L0_attn_gamma,
           L0_ffn_weight):
    x = np.asarray(L0_attn_x, dtype=np.float32)
    B, L, d = x.shape
    n_chunks = L // 2 // TC
    nc = _get_nc(n_chunks)
    in_maps = make_inputs(x, L0_attn_conv_w, L0_attn_conv_b, L0_attn_W_r,
                          L0_attn_b_r, L0_attn_W_i, L0_attn_b_i, L0_attn_log_a,
                          L0_attn_W_out, L0_attn_gamma, L0_ffn_weight)
    res = run_bass_kernel_spmd(nc, in_maps, core_ids=list(range(8)))
    half = L // 2
    out = np.empty((B, L, d), np.float32)
    for c in range(8):
        b, h = c // 2, c % 2
        out[b, h * half:(h + 1) * half] = res.results[c]["out"]
    return out


# version 07
# speedup vs baseline: 1.1175x; 1.0974x over previous
import numpy as np
import ml_dtypes
import concourse.bass as bass
import concourse.mybir as mybir
from concourse.bass_utils import run_bass_kernel_spmd
from concourse.bass_isa import ReduceOp

F32 = mybir.dt.float32
BF16 = mybir.dt.bfloat16
AF = mybir.ActivationFunctionType
ALU = mybir.AluOpType

D = 1024
NF = 8
TC = 512
NTT = TC // 128

ENGS = ("sync", "gpsimd", "vector", "scalar", "tensor")


class Prog:

    def __init__(self):
        self.ops = {e: [] for e in ENGS}
        self.sems = {}
        self.prod_at = {e: {} for e in ENGS}
        self.waited = {e: {} for e in ENGS}

    def sem(self, name, producer, step=1):
        self.sems[name] = dict(p=producer, c=0, step=step)

    def mark(self, name):
        return self.sems[name]["c"]

    def add(self, eng, fn, waits=(), inc=None, inc_n=None, fwaits=()):
        w = []
        for (s, v) in waits:
            if v is None or v <= 0:
                continue
            sem = self.sems[s]
            if sem["p"] == eng and self.prod_at[eng].get(s, 0) >= v:
                continue
            if self.waited[eng].get(s, 0) >= v:
                continue
            self.waited[eng][s] = v
            w.append((s, v))
        for (s, v) in fwaits:
            if v is None or v <= 0:
                continue
            if self.waited[eng].get(s, 0) >= v:
                continue
            self.waited[eng][s] = v
            w.append((s, v))
        incs = []
        if inc is not None:
            n = inc_n if inc_n is not None else self.sems[inc]["step"]
            self.sems[inc]["c"] += n
            incs.append((inc, n))
            if self.sems[inc]["p"] == eng:
                self.prod_at[eng][inc] = self.sems[inc]["c"]
        self.ops[eng].append((w, fn, incs))
        return self.mark(inc) if inc else None

    def emit(self, nc):
        import contextlib
        with contextlib.ExitStack() as st:
            handles = {}
            for name in self.sems:
                handles[name] = st.enter_context(nc.semaphore(name))
            for name in getattr(self, "extra_sems", []):
                handles[name] = st.enter_context(nc.semaphore(name))
            self.handles = handles
            block = st.enter_context(nc.Block())

            def run(eng_name, eng):
                for (w, fn, incs) in self.ops[eng_name]:
                    for (s, v) in w:
                        eng.wait_ge(handles[s], v)
                    inst = fn(eng)
                    for (s, n) in incs:
                        inst.then_inc(handles[s], n)

            @block.sync
            def _(sync):
                run("sync", sync)

            @block.gpsimd
            def _(gpsimd):
                run("gpsimd", gpsimd)

            @block.vector
            def _(vector):
                run("vector", vector)

            @block.scalar
            def _(scalar):
                run("scalar", scalar)

            @block.tensor
            def _(tensor):
                run("tensor", tensor)


class Banks:

    def __init__(self, prog):
        self.p = prog
        self.state = {}

    def reg(self, name, tensor):
        self.state[name] = dict(t=tensor, readers=[], writer=None)

    def t(self, name):
        return self.state[name]["t"]

    def write_waits(self, name):
        st = self.state[name]
        w = list(st["readers"])
        st["readers"] = []
        return w

    def wrote(self, name, sem, val):
        self.state[name]["writer"] = (sem, val)

    def read_waits(self, name):
        st = self.state[name]
        return [st["writer"]] if st["writer"] else []

    def read(self, name, sem, val):
        self.state[name]["readers"].append((sem, val))


def build(n_chunks=16, debug=False):
    nc = bass.Bass("TRN2", target_bir_lowering=False)
    TOK = n_chunks * TC

    x_d = nc.dram_tensor("x", [TOK + 3, D], F32, kind="ExternalInput")
    wrT_d = nc.dram_tensor("wrT", [D, D], F32, kind="ExternalInput")
    wiT_d = nc.dram_tensor("wiT", [D, D], F32, kind="ExternalInput")
    woT_d = nc.dram_tensor("woT", [D, D], F32, kind="ExternalInput")
    wfT_d = nc.dram_tensor("wfT", [D, D], F32, kind="ExternalInput")
    convd_d = nc.dram_tensor("convd", [32, 128, 128], F32, kind="ExternalInput")
    br_d = nc.dram_tensor("br_fm", [128, NF], F32, kind="ExternalInput")
    bi_d = nc.dram_tensor("bi_fm", [128, NF], F32, kind="ExternalInput")
    cb_d = nc.dram_tensor("cb_fm", [128, NF], F32, kind="ExternalInput")
    la_d = nc.dram_tensor("la_fm", [128, NF], F32, kind="ExternalInput")
    gam_d = nc.dram_tensor("gam_fm", [128, NF], F32, kind="ExternalInput")
    consts_d = nc.dram_tensor("consts", [128, 8], F32, kind="ExternalInput")
    eye_d = nc.dram_tensor("eye", [128, 128], F32, kind="ExternalInput")
    eyebf_d = nc.dram_tensor("eye_bf", [128, 128], BF16, kind="ExternalInput")
    ones_d = nc.dram_tensor("ones_bf", [128, 1], BF16, kind="ExternalInput")
    rmask_d = nc.dram_tensor("rmask", [128, NF], F32, kind="ExternalInput")
    out_d = nc.dram_tensor("out", [TOK, D], F32, kind="ExternalOutput")
    h_scr = nc.dram_tensor("h_scr", [n_chunks, 128, NF * TC], BF16)
    a_scr = nc.dram_tensor("a_scr", [n_chunks, 128, NF * TC], BF16)
    g_scr = nc.dram_tensor("g_scr", [n_chunks, 128, NF * TC], BF16)
    sh_x = nc.dram_tensor("sh_x", [2, 128, NF], F32, addr_space="Shared")
    sh_f = nc.dram_tensor("sh_f", [2, 16], mybir.dt.uint32, addr_space="Shared")
    slot_d = nc.dram_tensor("slot", [1, 4], mybir.dt.uint32, kind="ExternalInput")
    nonce_d = nc.dram_tensor("nonce", [1, 1], mybir.dt.uint32, kind="ExternalInput")

    A = nc.alloc_sbuf_tensor
    wr_bf = A("wr_bf", [128, NF * D], BF16)
    wi_bf = A("wi_bf", [128, NF * D], BF16)
    wo_bf = A("wo_bf", [128, NF * D], BF16)
    wf_bf = A("wf_bf", [128, NF * D], BF16)
    conv_bf = A("conv_bf", [128, 32 * 128], BF16)

    x_ln = A("x_ln", [128, 4, D], F32)
    xn_tok = A("xn_tok", [128, NTT * D], BF16)
    xn_feat = A("xn_feat", [128, NF, 515], BF16)
    xcb = A("xcb", [128, NF * TC], BF16)
    b_t = A("b_t", [128, NF * TC], F32)
    a_t = A("a_t", [128, NF * TC], F32)
    h_t = A("h_t", [128, NF * TC], F32)
    sct = A("sct", [128, NF * TC], BF16)
    A_t = A("A_t", [128, NF * TC], F32)
    g_t = A("g_t", [128, NF * TC], BF16)
    zeros_t = A("zeros_t", [128, TC], F32)
    gat = A("gat", [128, NF], F32)
    rmask = A("rmask_s", [128, NF], F32)
    slot_s = A("slot_s", [1, 4], mybir.dt.uint32)
    nonce_s = A("nonce_s", [1, 1], mybir.dt.uint32)
    flag_s = A("flag_s", [1, 1], mybir.dt.uint32)
    hent = A("hent", [128, NF], F32)
    acar = A("acar", [128, NF], F32)
    sq_rot = A("sq_rot", [128, 2, 256], BF16)
    rt_rot = A("rt_rot", [128, 512], F32)
    it_rot = A("it_rot", [128, 2, 512], BF16)
    tt_rot = A("tt_rot", [128, 2, 512], BF16)
    ff_rot = A("ff_rot", [128, 2, 512], BF16)
    carry = A("carry", [128, NF], F32)
    spc = A("spc", [128, 1], F32)

    st6 = A("st6", [128, 2, 12], F32)
    mv = A("mv", [128, 4, 2], F32)
    sd_t = A("sd_t", [128, 4], F32)
    rs_t = A("rs_t", [128, 4], F32)
    nb_t = A("nb_t", [128, 4], F32)
    rms_row = A("rms_row", [1, 256], F32)
    rms_tok = A("rms_tok", [128, 4], F32)

    br2 = A("br2", [128, NF], F32)
    bi2 = A("bi2", [128, NF], F32)
    cb_fm = A("cb_fm_s", [128, NF], F32)
    c4 = A("c4", [128, NF], F32)
    hgam = A("hgam", [128, NF], F32)
    la_fm = A("la_fm_s", [128, NF], F32)
    K = A("K_s", [128, 8], F32)
    eye = A("eye_s", [128, 128], F32)
    eye_bf = A("eye_bf_s", [128, 128], BF16)
    ones_bf = A("ones_bf_s", [128, 1], BF16)
    colsums = A("colsums", [128, NF], F32)
    allred = A("allred", [128, NF], F32)
    qsc = A("qsc", [128, 2], F32)

    def wfs_slot(q):
        return sct[:, (q % 2) * 2048:(q % 2) * 2048 + 2048].bitcast(F32)
    C_EPS5, C_HALF, C_QRT, C_NQRT, C_EPS6, C_I1024, C_NEG1 = range(7)

    def KC(i):
        return K[:, i:i + 1]

    pb = {}
    for nm in ("pA1", "pA2", "pB", "pC", "pD", "pE", "pF", "pG"):
        pb[nm] = nc.alloc_psum_tensor(nm, [128, 512], F32)

    P = Prog()
    P.extra_sems = ["pollq", "exdq"]
    BK = Banks(P)
    for nm, t in pb.items():
        BK.reg(nm, t)

    for nm, prod, step in [
        ("dX", "sync", 16), ("dXR", "sync", 16), ("dOUT", "sync", 16),
        ("dI", "sync", 16), ("dW", "gpsimd", 16), ("dF", "sync", 16),
        ("Vmv", "vector", 1), ("Ssd", "scalar", 1), ("Vln", "vector", 1),
        ("TxnT", "tensor", 1), ("VxnE", "vector", 1),
        ("Tconv", "tensor", 1), ("Vxcb", "vector", 1),
        ("Tr", "tensor", 1), ("SaE", "scalar", 1),
        ("Ti", "tensor", 1), ("SiE", "scalar", 1), ("Vb1", "vector", 1),
        ("Vam", "vector", 1), ("Ssct", "scalar", 1),
        ("Vscan", "vector", 1), ("Vcar", "vector", 1), ("Vhbf", "vector", 1),
        ("Tffn", "tensor", 1), ("SfE", "scalar", 1), ("Vg", "vector", 1),
        ("Tw", "tensor", 1), ("SwoE", "scalar", 1), ("Ssq", "scalar", 1),
        ("Tones", "tensor", 1), ("Srms", "scalar", 1), ("Vrecip", "vector", 1),
        ("TrmsT", "tensor", 1), ("Vrtok", "vector", 1), ("Vt2", "vector", 1),
        ("TsgT", "tensor", 1), ("Vfin", "vector", 1),
        ("Vq", "vector", 1), ("Gq", "gpsimd", 1), ("Tq", "tensor", 1),
        ("Sln", "scalar", 1), ("dSH", "gpsimd", 16), ("dSR", "sync", 16),
        ("dEX", "gpsimd", 16), ("Gcc", "gpsimd", 1), ("Vhe", "vector", 1),
        ("Vht", "vector", 1), ("VAc", "vector", 1),
        ("Sinit", "scalar", 1),
        ("Vinit", "vector", 1),
    ]:
        P.sem(nm, prod, step)

    init_loads = [
        (K, consts_d), (eye, eye_d), (eye_bf, eyebf_d), (ones_bf, ones_d),
        (br2, br_d), (bi2, bi_d), (cb_fm, cb_d), (la_fm, la_d), (hgam, gam_d),
        (rmask, rmask_d), (slot_s, slot_d), (nonce_s, nonce_d),
    ]
    for sb, dr in init_loads:
        P.add("sync", lambda e, sb=sb, dr=dr: e.dma_start(out=sb[:, :], in_=dr[:, :]),
              inc="dI")
    m_dI_small = P.mark("dI")

    def wload(sb, dr):
        return P.add("gpsimd",
                     lambda e, sb=sb, dr=dr: e.dma_start(
                         out=bass.AP(sb, 0, [[NF * D, 128], [D, NF], [1, D]]),
                         in_=bass.AP(dr, 0, [[D, 128], [128 * D, NF], [1, D]])),
                     inc="dW")
    m_wr = wload(wr_bf, wrT_d)
    m_wi = wload(wi_bf, wiT_d)
    m_wo = wload(wo_bf, woT_d)
    m_conv = P.add("gpsimd",
                   lambda e: e.dma_start(
                       out=bass.AP(conv_bf, 0, [[32 * 128, 128], [128, 32], [1, 128]]),
                       in_=bass.AP(convd_d, 0, [[128, 128], [128 * 128, 32], [1, 128]])),
                   inc="dW")

    P.add("scalar",
          lambda e: e.activation(c4[:, :], la_fm[:, :], AF.Sigmoid),
          waits=[("dI", m_dI_small)], inc="Sinit")
    P.add("scalar",
          lambda e: e.activation(c4[:, :], c4[:, :], AF.Ln),
          inc="Sinit")
    m_sp = P.mark("Sinit")
    P.add("vector", lambda e: e.tensor_scalar_mul(c4[:, :], c4[:, :], 4.0),
          waits=[("Sinit", m_sp)], inc="Vinit")
    P.add("vector", lambda e: e.tensor_scalar_mul(br2[:, :], br2[:, :], 0.5),
          waits=[("dI", m_dI_small)], inc="Vinit")
    P.add("vector", lambda e: e.tensor_scalar_mul(bi2[:, :], bi2[:, :], 0.5),
          inc="Vinit")
    P.add("vector", lambda e: e.tensor_scalar_mul(hgam[:, :], hgam[:, :], 0.5),
          inc="Vinit")
    P.add("vector", lambda e: e.memset(zeros_t[:, :], 0.0), inc="Vinit")
    P.add("vector", lambda e: e.memset(gat[:, :], 0.0), inc="Vinit")
    m_vinit = P.mark("Vinit")

    mF = []
    for q in range(NF):
        mF.append(P.add("sync",
                        lambda e, q=q: e.dma_start(out=wfs_slot(q),
                                                   in_=wfT_d[q * 128:(q + 1) * 128, :]),
                        waits=[("Vq", q - 1)],
                        inc="dF"))
        P.add("vector",
              lambda e, q=q: e.tensor_reduce(colsums[:, q:q + 1], wfs_slot(q),
                                             mybir.AxisListType.X, ALU.add,
                                             apply_absolute_value=True),
              waits=[("dF", 16 * (q + 1))], inc="Vq")
    m_abs = P.mark("Vq")
    P.add("vector", lambda e: e.tensor_reduce(allred[:, 0:1], colsums[:, :],
                                              mybir.AxisListType.X, ALU.add),
          fwaits=[("Vq", m_abs)], inc="Vq")
    ww = BK.write_waits("pD")
    P.add("tensor",
          lambda e: e.transpose(pb["pD"][0:1, 0:128], allred[:, 0:1], eye[:, :]),
          waits=ww + [("Vq", P.mark("Vq")), ("dI", m_dI_small)], inc="Tq")
    BK.wrote("pD", "Tq", P.mark("Tq"))
    P.add("vector",
          lambda e: e.tensor_copy(rms_row[0:1, 0:128], pb["pD"][0:1, 0:128]),
          waits=BK.read_waits("pD"), inc="Vq")
    BK.read("pD", "Vq", P.mark("Vq"))
    P.add("vector", lambda e: e.tensor_reduce(st6[0:1, 0, 0:1], rms_row[0:1, 0:128],
                                              mybir.AxisListType.X, ALU.add),
          fwaits=[("Vq", P.mark("Vq"))], inc="Vq")
    P.add("vector", lambda e: e.tensor_copy(
        rms_row[0:1, 128:256], st6[0:1, 0, 0:1].to_broadcast((1, 128))),
          fwaits=[("Vq", P.mark("Vq"))], inc="Vq")
    ww = BK.write_waits("pD")
    P.add("tensor",
          lambda e: e.transpose(pb["pD"][:, 256:257], rms_row[0:1, 128:256],
                                eye[0:1, 0:1]),
          waits=ww + [("Vq", P.mark("Vq"))], inc="Tq")
    BK.wrote("pD", "Tq", P.mark("Tq"))
    P.add("vector", lambda e: e.tensor_copy(allred[:, 1:2], pb["pD"][:, 256:257]),
          waits=BK.read_waits("pD"), inc="Vq")
    BK.read("pD", "Vq", P.mark("Vq"))
    P.add("vector", lambda e: e.tensor_scalar_mul(qsc[:, 0:1], allred[:, 1:2],
                                                  1.0 / (D * D)),
          fwaits=[("Vq", P.mark("Vq"))], inc="Vq")
    P.add("vector", lambda e: e.tensor_scalar_max(qsc[:, 0:1], qsc[:, 0:1], 1e-5),
          fwaits=[("Vq", P.mark("Vq"))], inc="Vq")
    P.add("vector", lambda e: e.reciprocal(qsc[:, 1:2], qsc[:, 0:1]),
          fwaits=[("Vq", P.mark("Vq"))], inc="Vq")
    m_qsc = P.mark("Vq")
    qt, q1, q2 = a_t[:, 0:D], a_t[:, D:2 * D], a_t[:, 2 * D:3 * D]
    for q in range(NF):
        P.add("sync",
              lambda e, q=q: e.dma_start(out=wfs_slot(q),
                                         in_=wfT_d[q * 128:(q + 1) * 128, :]),
              waits=[("Vq", m_qsc if q < 2 else m_qsc + 5 * (q - 1))],
              inc="dF")
        P.add("vector",
              lambda e, q=q: e.tensor_scalar_mul(qt, wfs_slot(q), qsc[:, 1:2]),
              waits=[("dF", 16 * (NF + q + 1))],
              fwaits=[("Vq", m_qsc)], inc="Vq")
        P.add("vector",
              lambda e: e.tensor_single_scalar(q1, qt, 0.5, ALU.is_ge), inc="Vq")
        P.add("vector",
              lambda e: e.tensor_single_scalar(q2, qt, -0.5, ALU.is_le), inc="Vq")
        P.add("vector", lambda e: e.tensor_sub(q1, q1, q2), inc="Vq")
        P.add("vector",
              lambda e, q=q: e.tensor_scalar_mul(wf_bf[:, q * D:(q + 1) * D], q1,
                                                 qsc[:, 0:1]),
              inc="Vq")
    m_quant = P.mark("Vq")

    m_ln = {}
    m_fin = {}
    m_dout = {}
    m_dxr = {}
    m_scanC = {}
    m_bmul = {}
    m_car = {}
    m_acar = {}
    m_xnE = {}
    m_xcb = {}
    m_xcball = {}
    m_aAll = {}
    m_b1All = {}
    m_hscan = {}
    m_am = {}
    m_convAll = {}
    m_store = {}
    m_gstore = {}
    m_read = {}
    m_ht = {}

    rotbank_fr = ["pD", "pE"]
    fr_idx = [0]

    def next_fr():
        b = rotbank_fr[fr_idx[0] % 2]
        fr_idx[0] += 1
        return b

    def OH(c0, c1):
        return sct[:, 2 * c0:2 * c1].bitcast(F32)

    def h_slot(p):
        return a_t[:, p * 2048:(p + 1) * 2048].bitcast(BF16)

    def A_slot(p):
        return b_t[:, p * 2048:(p + 1) * 2048].bitcast(BF16)

    def g_slot(p, lo=0, hi=NF * TC):
        return h_t[:, p * 2048 + lo // 2: p * 2048 + hi // 2].bitcast(BF16)

    P.add("sync", lambda e: e.dma_start(out=x_ln[0:3, 1, :], in_=x_d[0:3, :]),
          inc="dI")
    m_dlead = P.mark("dI")
    P.add("vector", lambda e: e.bn_stats(st6[0:3, 1, 0:6], x_ln[0:3, 1, 0:512]),
          waits=[("dI", m_dlead)], inc="Vmv")
    P.add("vector", lambda e: e.bn_stats(st6[0:3, 1, 6:12], x_ln[0:3, 1, 512:1024]),
          inc="Vmv")
    m_lst = P.mark("Vmv")
    P.add("vector", lambda e: e.bn_aggr(mv[0:3, 0, :], st6[0:3, 1, 0:12]),
          fwaits=[("Vmv", m_lst)], inc="Vmv")
    m_lag = P.mark("Vmv")
    P.add("scalar", lambda e: e.activation(sd_t[0:3, 0:1], mv[0:3, 0, 1:2],
                                           AF.Sqrt, bias=K[0:3, C_EPS5:C_EPS5 + 1]),
          waits=[("Vmv", m_lag), ("dI", m_dI_small)], inc="Ssd")
    m_lsd = P.mark("Ssd")
    P.add("vector", lambda e: e.reciprocal(rs_t[0:3, 0:1], sd_t[0:3, 0:1]),
          waits=[("Ssd", m_lsd)], inc="Vln")
    m_lrs = P.mark("Vln")
    P.add("vector", lambda e: e.scalar_tensor_tensor(nb_t[0:3, 0:1], mv[0:3, 0, 0:1],
                                                     -1.0, rs_t[0:3, 0:1],
                                                     ALU.mult, ALU.mult),
          fwaits=[("Vln", m_lrs)], inc="Vln")
    m_lnb = P.mark("Vln")
    P.add("scalar", lambda e: e.activation(g_t[0:3, 0:1024], x_ln[0:3, 1, :],
                                           AF.Identity, bias=nb_t[0:3, 0:1],
                                           scale=rs_t[0:3, 0:1]),
          waits=[("Vln", m_lnb)], inc="Sln")
    m_lead = P.mark("Sln")
    for j in range(NF):
        bank = "pA1" if j % 2 == 0 else "pA2"
        ww = BK.write_waits(bank)
        P.add("tensor",
              lambda e, j=j, bank=bank: e.transpose(
                  BK.t(bank)[:, 0:2].bitcast(BF16)[:, 0:3],
                  g_t[0:3, j * 128:(j + 1) * 128],
                  eye_bf[0:3, 0:3]),
              waits=ww + [("Sln", m_lead), ("dI", m_dI_small)], inc="TxnT")
        BK.wrote(bank, "TxnT", P.mark("TxnT"))
        P.add("vector",
              lambda e, j=j, bank=bank: e.tensor_copy(
                  xn_feat[:, j, 0:3], BK.t(bank)[:, 0:2].bitcast(BF16)[:, 0:3]),
              waits=BK.read_waits(bank), inc="VxnE")
        BK.read(bank, "VxnE", P.mark("VxnE"))

    m_ag = {}

    def emit_lnload_A(k):
        for t in range(NTT):
            g = 4 * k + t
            xw = [("Sln", m_ln.get(g - 4, 0))]
            if g == 1:
                xw.append(("Sln", m_lead))
            P.add("sync",
                  lambda e, g=g: e.dma_start(
                      out=x_ln[:, g % 4, :],
                      in_=x_d[3 + g * 128:3 + (g + 1) * 128, :]),
                  waits=xw, inc="dX")
            P.add("vector",
                  lambda e, t=t, g=g: e.bn_stats(st6[:, t % 2, 0:6],
                                                 x_ln[:, g % 4, 0:512]),
                  waits=[("dX", 16 * (g + 1))], inc="Vmv")
            P.add("vector",
                  lambda e, t=t, g=g: e.bn_stats(st6[:, t % 2, 6:12],
                                                 x_ln[:, g % 4, 512:1024]),
                  inc="Vmv")
            m_st = P.mark("Vmv")
            P.add("vector",
                  lambda e, t=t: e.bn_aggr(mv[:, t, :], st6[:, t % 2, 0:12]),
                  fwaits=[("Vmv", m_st)], inc="Vmv")
            m_ag[(k, t)] = P.mark("Vmv")

    def emit_lnload_B(k):
        for t in range(NTT):
            P.add("scalar",
                  lambda e, t=t: e.activation(sd_t[:, t:t + 1], mv[:, t, 1:2],
                                              AF.Sqrt, bias=KC(C_EPS5)),
                  waits=[("Vmv", m_ag[(k, t)])], inc="Ssd")
        m_sd4 = P.mark("Ssd")
        P.add("vector",
              lambda e: e.reciprocal(rs_t[:, 0:4], sd_t[:, 0:4]),
              waits=[("Ssd", m_sd4)], inc="Vln")
        m_rs = P.mark("Vln")
        P.add("vector",
              lambda e: e.scalar_tensor_tensor(nb_t[:, 0:4],
                                               bass.AP(mv, 0, [[8, 128], [2, 4]]),
                                               -1.0, rs_t[:, 0:4],
                                               ALU.mult, ALU.mult),
              fwaits=[("Vln", m_rs)], inc="Vln")
        m_nb = P.mark("Vln")
        for t in range(NTT):
            g = 4 * k + t
            m_ln[g] = P.add("scalar",
                            lambda e, g=g, t=t: e.activation(
                                xn_tok[:, t * D:(t + 1) * D], x_ln[:, g % 4, :],
                                AF.Identity, bias=nb_t[:, t:t + 1],
                                scale=rs_t[:, t:t + 1]),
                            waits=[("TxnT", P.mark("TxnT")), ("Vln", m_nb)],
                            inc="Sln")

    def emit_front_head(k):
        if k > 0:
            P.add("vector",
                  lambda e: e.tensor_copy(xn_feat[:, :, 0:3], xn_feat[:, :, 512:515]),
                  waits=[("Tconv", m_convAll[k - 1])], inc="Vln")

        for j in range(NF):
            bank = "pA1" if j % 2 == 0 else "pA2"
            ww = BK.write_waits(bank)
            for t in range(NTT):
                P.add("tensor",
                      lambda e, j=j, t=t, bank=bank: e.transpose(
                          BK.t(bank)[:, 0:256].bitcast(BF16)[:, t * 128:(t + 1) * 128],
                          xn_tok[:, t * D + j * 128: t * D + (j + 1) * 128],
                          eye_bf[:, :]),
                      waits=(ww if t == 0 else []) + [("Sln", m_ln[4 * k + t])],
                      inc="TxnT" if t == NTT - 1 else None)
            BK.wrote(bank, "TxnT", P.mark("TxnT"))
            m_xnE[(k, j)] = P.add(
                "vector",
                lambda e, j=j, bank=bank: e.tensor_copy(
                    xn_feat[:, j, 3:515], BK.t(bank)[:, 0:256].bitcast(BF16)),
                waits=BK.read_waits(bank) + [("Vinit", m_vinit),
                                             ("Tffn", P.mark("Tffn"))],
                inc="VxnE")
            BK.read(bank, "VxnE", m_xnE[(k, j)])

        for j in range(NF):
            bank = "pB" if j % 2 == 0 else "pC"
            ww = BK.write_waits(bank)
            for tap in range(4):
                P.add("tensor",
                      lambda e, j=j, tap=tap, bank=bank: e.matmul(
                          BK.t(bank)[:, 0:512],
                          conv_bf[:, (j * 4 + tap) * 128:(j * 4 + tap + 1) * 128],
                          xn_feat[:, j, tap:tap + 512],
                          start=(tap == 0), stop=(tap == 3)),
                      waits=(ww if tap == 0 else []) + [("VxnE", m_xnE[(k, j)]),
                                                        ("dW", m_conv)],
                      inc="Tconv" if tap == 3 else None)
            BK.wrote(bank, "Tconv", P.mark("Tconv"))
            if j % 2 == 0:
                m_xcb[(k, j)] = P.add(
                    "scalar",
                    lambda e, j=j, bank=bank: e.activation(
                        xcb[:, j * TC:(j + 1) * TC], BK.t(bank)[:, 0:512],
                        AF.Identity, bias=cb_fm[:, j:j + 1]),
                    waits=BK.read_waits(bank) + [("Vb1", P.mark("Vb1"))],
                    inc="Vxcb", )
                BK.read(bank, "Vxcb", m_xcb[(k, j)])
            else:
                m_xcb[(k, j)] = P.add(
                    "vector",
                    lambda e, j=j, bank=bank: e.tensor_scalar(
                        xcb[:, j * TC:(j + 1) * TC], BK.t(bank)[:, 0:512],
                        cb_fm[:, j:j + 1], None, ALU.add),
                    waits=BK.read_waits(bank) + [("Ti", P.mark("Ti")),
                                                 ("Vb1", P.mark("Vb1"))], inc="Vxcb")
                BK.read(bank, "Vxcb", m_xcb[(k, j)])
        m_convAll[k] = P.mark("Tconv")
        m_xcball[k] = P.mark("Vxcb")


    def emit_front_mid(k):
        for j in range(NF):
            bank = next_fr()
            ww = BK.write_waits(bank)
            for q in range(NF):
                P.add("tensor",
                      lambda e, j=j, q=q, bank=bank: e.matmul(
                          BK.t(bank)[:, 0:512],
                          wr_bf[:, q * D + j * 128: q * D + (j + 1) * 128],
                          xcb[:, q * TC:(q + 1) * TC],
                          start=(q == 0), stop=(q == NF - 1)),
                      waits=(ww if q == 0 else []) + [("Vxcb", m_xcball[k]), ("dW", m_wr)],
                      inc="Tr" if q == NF - 1 else None)
            BK.wrote(bank, "Tr", P.mark("Tr"))
            P.add("scalar",
                  lambda e, j=j, bank=bank: e.activation(
                      rt_rot[:, :], BK.t(bank)[:, 0:512], AF.Tanh,
                      bias=br2[:, j:j + 1], scale=KC(C_HALF)),
                  waits=BK.read_waits(bank) + [("Vinit", m_vinit)], inc="SaE")
            BK.read(bank, "SaE", P.mark("SaE"))
            P.add("scalar",
                  lambda e, j=j: e.activation(
                      a_t[:, j * TC:(j + 1) * TC], rt_rot[:, :], AF.Exp,
                      bias=c4[:, j:j + 1], scale=c4[:, j:j + 1]),
                  waits=[("Vscan", m_hscan.get((k - 1, j), 0)),
                         ("VAc", m_acar.get(k - 1, 0)),
                         ("Vq", m_quant if k == 0 else 0)],
                  inc="SaE")
        m_aAll[k] = P.mark("SaE")
        if k + 1 < n_chunks:
            emit_lnload_A(k + 1)

        for j in range(NF):
            bank = next_fr()
            ww = BK.write_waits(bank)
            for q in range(NF):
                P.add("tensor",
                      lambda e, j=j, q=q, bank=bank: e.matmul(
                          BK.t(bank)[:, 0:512],
                          wi_bf[:, q * D + j * 128: q * D + (j + 1) * 128],
                          xcb[:, q * TC:(q + 1) * TC],
                          start=(q == 0), stop=(q == NF - 1)),
                      waits=(ww if q == 0 else []) + [("dW", m_wi)],
                      inc="Ti" if q == NF - 1 else None)
            BK.wrote(bank, "Ti", P.mark("Ti"))
            P.add("scalar",
                  lambda e, j=j, bank=bank: e.activation(
                      it_rot[:, j % 2, :], BK.t(bank)[:, 0:512], AF.Tanh,
                      bias=bi2[:, j:j + 1], scale=KC(C_HALF)),
                  waits=BK.read_waits(bank) + [("Vb1", max(0, P.mark("Vb1") - 1))],
                  inc="SiE")
            BK.read(bank, "SiE", P.mark("SiE"))
            m_sie = P.mark("SiE")
            P.add("vector",
                  lambda e, j=j: e.scalar_tensor_tensor(
                      b_t[:, j * TC:(j + 1) * TC], it_rot[:, j % 2, :], 1.0,
                      xcb[:, j * TC:(j + 1) * TC], ALU.add, ALU.mult),
                  waits=[("SiE", m_sie)], inc="Vb1")
        m_b1All[k] = P.mark("Vb1")

        P.add("vector",
              lambda e: e.tensor_scalar_min(a_t[:, :], a_t[:, :], 1.0),
              waits=[("SaE", m_aAll[k])], inc="Vam")
        m_am[k] = P.mark("Vam")
        for j in range(NF):
            init_a = 1.0 if k == 0 else acar[:, j:j + 1]
            P.add("vector",
                  lambda e, j=j, init_a=init_a: e.tensor_tensor_scan(
                      A_t[:, j * TC:(j + 1) * TC], a_t[:, j * TC:(j + 1) * TC],
                      zeros_t[:, :], init_a, ALU.mult, ALU.add),
                  waits=([("dSH", m_store.get(k - 1, (0, 0))[1])] if j == 0 else []),
                  inc="Vscan")
        m_acar[k] = P.add("vector",
                          lambda e: e.tensor_copy(
                              bass.AP(acar, 0, [[NF, 128], [1, NF], [1, 1]]),
                              bass.AP(A_t, TC - 1, [[NF * TC, 128], [TC, NF], [1, 1]])),
                          fwaits=[("Vscan", P.mark("Vscan"))], inc="VAc")
        P.add("gpsimd", lambda e, k=k: e.dma_start(out=a_scr[k, :, :], in_=A_t[:, :]),
              waits=[("VAc", m_acar[k])], inc="dSH")

        for j in range(NF):
            bank = next_fr()
            ww = BK.write_waits(bank)
            for q in range(NF):
                P.add("tensor",
                      lambda e, j=j, q=q, bank=bank: e.matmul(
                          BK.t(bank)[:, 0:512],
                          wf_bf[:, q * D + j * 128: q * D + (j + 1) * 128],
                          xn_feat[:, q, 3:515],
                          start=(q == 0), stop=(q == NF - 1)),
                      waits=(ww if q == 0 else []) + [("Vq", m_quant),
                                                      ("VxnE", m_xnE[(k, NF - 1)])],
                      inc="Tffn" if q == NF - 1 else None)
            BK.wrote(bank, "Tffn", P.mark("Tffn"))
            P.add("scalar",
                  lambda e, j=j, bank=bank: e.activation(
                      tt_rot[:, j % 2, :], BK.t(bank)[:, 0:512], AF.Tanh,
                      scale=KC(C_HALF)),
                  waits=BK.read_waits(bank) + [("Vg", max(0, P.mark("Vg") - 1))],
                  inc="SfE")
            P.add("scalar",
                  lambda e, j=j, bank=bank: e.activation(
                      ff_rot[:, j % 2, :], BK.t(bank)[:, 0:512], AF.Identity,
                      scale=hgam[:, j:j + 1]),
                  inc="SfE")
            BK.read(bank, "SfE", P.mark("SfE"))
            m_fe = P.mark("SfE")
            P.add("vector",
                  lambda e, j=j: e.scalar_tensor_tensor(
                      g_t[:, j * TC:(j + 1) * TC], tt_rot[:, j % 2, :], 1.0,
                      ff_rot[:, j % 2, :], ALU.add, ALU.mult),
                  waits=[("SfE", m_fe), ("dSH", m_gstore.get(k - 1, 0))], inc="Vg")
        mg = P.mark("Vg")
        P.add("gpsimd", lambda e, k=k: e.dma_start(out=g_scr[k, :, :], in_=g_t[:, :]),
              waits=[("Vg", mg)], inc="dSH")
        m_gstore[k] = P.mark("dSH")


    def emit_front_tail(k):
        P.add("scalar",
              lambda e: e.activation(h_t[:, :], a_t[:, :], AF.Square),
              waits=[("Vam", m_am[k]), ("dSH", m_store.get(k - 1, (0, 0))[1]),
                     ("Vcar", m_car.get(k - 1, 0))],
              inc="Ssct")
        P.add("scalar",
              lambda e: e.activation(sct[:, :], h_t[:, :], AF.Sqrt,
                                     bias=KC(C_QRT), scale=KC(C_NQRT)),
              waits=[("Vb1", m_bmul.get(k - 1, 0))],
              inc="Ssct")
        m_sct = P.mark("Ssct")
        P.add("vector",
              lambda e: e.tensor_mul(b_t[:, :], b_t[:, :], sct[:, :]),
              waits=[("Ssct", m_sct), ("Vb1", m_b1All[k])], inc="Vb1")
        m_bmul[k] = P.mark("Vb1")
        for j in range(NF):
            init_h = 0.0 if k == 0 else carry[:, j:j + 1]
            P.add("vector",
                  lambda e, j=j, init_h=init_h: e.tensor_tensor_scan(
                      h_t[:, j * TC:(j + 1) * TC], a_t[:, j * TC:(j + 1) * TC],
                      b_t[:, j * TC:(j + 1) * TC], init_h, ALU.mult, ALU.add),
                  waits=[("Ssct", m_sct)], inc="Vscan")
            m_hscan[(k, j)] = P.mark("Vscan")
        m_scanC[k] = P.mark("Vscan")
        m_car[k] = P.add("vector",
                         lambda e: e.tensor_copy(
                             bass.AP(carry, 0, [[NF, 128], [1, NF], [1, 1]]),
                             bass.AP(h_t, TC - 1, [[NF * TC, 128], [TC, NF], [1, 1]])),
                         fwaits=[("Vscan", m_scanC[k])], inc="Vcar")
        P.add("gpsimd", lambda e, k=k: e.dma_start(out=h_scr[k, :, :], in_=h_t[:, :]),
              waits=[("Vcar", m_car[k])], inc="dSH")
        m_store[k] = (P.mark("Vhbf"), P.mark("dSH"))


    def emit_exchange():
        def ex_fn(e):
            rs_r = e.alloc_register("exslot")
            rp_r = e.alloc_register("expslot")
            rn = e.alloc_register("exnonce")
            rf = e.alloc_register("exflag")
            rv = e.alloc_register("expollv")
            e.reg_load(rs_r, slot_s[0:1, 0:1])
            e.reg_load(rp_r, slot_s[0:1, 1:2])
            e.reg_load(rn, nonce_s[0:1, 0:1])
            rs = e.snap(rs_r, min_val=0, max_val=1)
            rp = e.snap(rp_r, min_val=0, max_val=1)
            pq = P.handles["pollq"]
            dq = P.handles["exdq"]
            e.dma_start(out=sh_x[bass.ds(rs, 1), :, :],
                        in_=carry[:, :]).then_inc(dq, 16)
            e.wait_ge(dq, 16)
            e.dma_start(out=sh_f[bass.ds(rs, 1), 0:1],
                        in_=nonce_s[0:1, :]).then_inc(dq, 16)
            e.wait_ge(dq, 32)
            rneed = e.alloc_register("exneed")
            e.reg_load(rneed, slot_s[0:1, 2:3])
            e.reg_mul(rn, rn, rneed)
            e.reg_mov(rf, 0)
            e.reg_mov(rv, 0)
            with e.While(lambda: e.snap(rf, donate=False)
                         != e.snap(rn, donate=False)):
                e.reg_add(rv, rv, 16)
                e.dma_start(out=flag_s[0:1, :],
                            in_=sh_f[bass.ds(rp, 1), 0:1]).then_inc(pq, 16)
                e.wait_ge(pq, rv)
                e.reg_load(rf, flag_s[0:1, 0:1])
            nv = e.snap(rneed, min_val=0, max_val=1)
            e.dma_start(out=gat[:, :],
                        in_=sh_x[bass.ds(rp, 1), :, :], cond=nv).then_inc(dq, 16)
            e.wait_ge(dq, 48)
            return e.nop()

        P.add("gpsimd", ex_fn,
              waits=[("Vcar", m_car[n_chunks - 1]), ("dI", m_dI_small)],
              inc="Gcc")
        P.add("vector", lambda e: e.tensor_mul(hent[:, :], gat[:, :], rmask[:, :]),
              waits=[("Gcc", 1)], inc="Vhe")

    m_vhe_done = [0]
    p1_final = []

    def emit_read(k):
        if k >= n_chunks:
            return
        p = k % 2
        w = [("dSH", m_store[k][1]), ("Vht", m_ht.get(k - 2, 0)),
             ("Tw", P.mark("Tw")), ("Vt2", P.mark("Vt2"))] + p1_final
        P.add("sync", lambda e, k=k, p=p: e.dma_start(out=h_slot(p), in_=h_scr[k, :, :]),
              waits=w, inc="dSR")
        P.add("sync", lambda e, k=k, p=p: e.dma_start(out=A_slot(p), in_=a_scr[k, :, :]),
              waits=[("dSH", m_store[k][1])] + p1_final, inc="dSR")
        P.add("sync", lambda e, k=k, p=p: e.dma_start(out=g_slot(p), in_=g_scr[k, :, :]),
              waits=[("dSH", m_gstore[k])] + p1_final, inc="dSR")
        m_read[k] = P.mark("dSR")

    def emit_back(k):
        p = k % 2
        for j in range(NF):
            P.add("vector",
                  lambda e, j=j, p=p: e.scalar_tensor_tensor(
                      xcb[:, j * TC:(j + 1) * TC],
                      A_slot(p)[:, j * TC:(j + 1) * TC],
                      hent[:, j:j + 1],
                      h_slot(p)[:, j * TC:(j + 1) * TC],
                      ALU.mult, ALU.add),
                  waits=[("dSR", m_read[k]),
                         ("Tw", P.mark("Tw"))] + p1_final,
                  fwaits=[("Vhe", m_vhe_done[0])],
                  inc="Vht")
        m_ht[k] = P.mark("Vht")

        for hh in range(2):
            for j in range(NF):
                bank = "pF" if j % 2 == 0 else "pG"
                ww = BK.write_waits(bank)
                for q in range(NF):
                    P.add("tensor",
                          lambda e, j=j, q=q, hh=hh, bank=bank: e.matmul(
                              BK.t(bank)[:, 0:256],
                              wo_bf[:, q * D + j * 128: q * D + (j + 1) * 128],
                              xcb[:, q * TC + hh * 256: q * TC + hh * 256 + 256],
                              start=(q == 0), stop=(q == NF - 1)),
                          waits=(ww if q == 0 else []) + [("Vht", m_ht[k]),
                                                          ("dW", m_wo)],
                          inc="Tw" if q == NF - 1 else None)
                BK.wrote(bank, "Tw", P.mark("Tw"))
                P.add("scalar",
                      lambda e, j=j, bank=bank: e.activation(
                          OH(j * 256, (j + 1) * 256), BK.t(bank)[:, 0:256],
                          AF.Identity),
                      waits=BK.read_waits(bank) + [("TsgT", P.mark("TsgT"))] + p1_final,
                      inc="SwoE")
                BK.read(bank, "SwoE", P.mark("SwoE"))
                P.add("scalar",
                      lambda e, j=j: e.activation(
                          sq_rot[:, j % 2, :], OH(j * 256, (j + 1) * 256),
                          AF.Square),
                      inc="Ssq")
                m_sq = P.mark("Ssq")
                wwB = BK.write_waits("pB") if j == 0 else []
                P.add("tensor",
                      lambda e, j=j, hh=hh: e.matmul(
                          pb["pB"][0:1, hh * 256:hh * 256 + 256],
                          ones_bf[:, 0:1], sq_rot[:, j % 2, :],
                          start=(j == 0), stop=(j == NF - 1)),
                      waits=wwB + [("Ssq", m_sq)],
                      inc="Tones" if j == NF - 1 else None)
            BK.wrote("pB", "Tones", P.mark("Tones"))
            P.add("scalar",
                  lambda e, hh=hh: e.activation(
                      rms_row[0:1, :], pb["pB"][0:1, hh * 256:hh * 256 + 256],
                      AF.Sqrt, bias=K[0:1, C_EPS6:C_EPS6 + 1],
                      scale=K[0:1, C_I1024:C_I1024 + 1]),
                  waits=BK.read_waits("pB") + [("Vrecip", P.mark("Vrecip")),
                                               ("TrmsT", P.mark("TrmsT"))],
                  inc="Srms")
            BK.read("pB", "Srms", P.mark("Srms"))
            m_srms = P.mark("Srms")
            P.add("vector", lambda e: e.reciprocal(rms_row[0:1, :], rms_row[0:1, :]),
                  waits=[("Srms", m_srms)], inc="Vrecip")
            m_rc = P.mark("Vrecip")
            for tt in range(2):
                tile = hh * 2 + tt
                ww = BK.write_waits("pC")
                P.add("tensor",
                      lambda e, tt=tt, tile=tile: e.transpose(
                          pb["pC"][:, 384 + tile:384 + tile + 1],
                          rms_row[0:1, tt * 128:(tt + 1) * 128],
                          eye[0:1, 0:1]),
                      waits=ww + [("Vrecip", m_rc)], inc="TrmsT")
                BK.wrote("pC", "TrmsT", P.mark("TrmsT"))
                m_tt = P.mark("TrmsT")
                P.add("vector",
                      lambda e, tile=tile: e.tensor_copy(
                          rms_tok[:, tile:tile + 1],
                          pb["pC"][:, 384 + tile:384 + tile + 1]),
                      waits=[("TrmsT", m_tt)], inc="Vrtok")
                BK.read("pC", "Vrtok", P.mark("Vrtok"))
            for j in range(NF):
                P.add("vector",
                      lambda e, j=j, hh=hh, p=p: e.tensor_tensor(
                          OH(j * 256, (j + 1) * 256),
                          OH(j * 256, (j + 1) * 256),
                          g_slot(p, j * TC + hh * 256, j * TC + hh * 256 + 256),
                          ALU.mult),
                      waits=[("SwoE", P.mark("SwoE")), ("dSR", m_read[k])],
                      inc="Vt2")
            m_t2 = P.mark("Vt2")
            for tt in range(2):
                tile = hh * 2 + tt
                g = 4 * k + tile
                pair = ("pA1", "pA2") if tt == 0 else ("pD", "pE")
                P.add("sync",
                      lambda e, g=g: e.dma_start(
                          out=x_ln[:, g % 4, :],
                          in_=x_d[3 + g * 128:3 + (g + 1) * 128, :]),
                      waits=[("dOUT", m_dout.get(g - 4, 0)),
                             ("Sln", m_ln[4 * n_chunks - 1])], inc="dXR")
                m_dxr[g] = P.mark("dXR")
                ww0 = BK.write_waits(pair[0])
                ww1 = BK.write_waits(pair[1])
                for j in range(NF):
                    bk = pair[0] if j < 4 else pair[1]
                    P.add("tensor",
                          lambda e, j=j, tt=tt, bk=bk: e.transpose(
                              BK.t(bk)[:, (j % 4) * 128:(j % 4) * 128 + 128],
                              OH(j * 256 + tt * 128, j * 256 + tt * 128 + 128),
                              eye[:, :]),
                          waits=(ww0 + ww1 if j == 0 else []) + [("Vt2", m_t2)],
                          inc="TsgT" if j == NF - 1 else None)
                BK.wrote(pair[0], "TsgT", P.mark("TsgT"))
                BK.wrote(pair[1], "TsgT", P.mark("TsgT"))
                m_sgt = P.mark("TsgT")
                for half_pb in range(2):
                    bk = pair[half_pb]
                    P.add("vector",
                          lambda e, g=g, tile=tile, bk=bk, half_pb=half_pb:
                          e.scalar_tensor_tensor(
                              x_ln[:, g % 4, half_pb * 512:(half_pb + 1) * 512],
                              BK.t(bk)[:, 0:512],
                              rms_tok[:, tile:tile + 1],
                              x_ln[:, g % 4, half_pb * 512:(half_pb + 1) * 512],
                              ALU.mult, ALU.add),
                          waits=[("TsgT", m_sgt), ("dXR", m_dxr[g])],
                          fwaits=[("Vrtok", P.mark("Vrtok"))],
                          inc="Vfin")
                    BK.read(bk, "Vfin", P.mark("Vfin"))
                m_fin[g] = P.mark("Vfin")
                m_dout[g] = P.add("sync",
                                  lambda e, g=g: e.dma_start(
                                      out=out_d[g * 128:(g + 1) * 128, :],
                                      in_=x_ln[:, g % 4, :]),
                                  waits=[("Vfin", m_fin[g])], inc="dOUT")

    emit_lnload_A(0)
    emit_lnload_B(0)
    emit_front_head(0)
    for k in range(n_chunks):
        emit_front_mid(k)
        if k + 1 < n_chunks:
            emit_lnload_B(k + 1)
            emit_front_head(k + 1)
        emit_front_tail(k)
    p1_final.extend([("Vscan", P.mark("Vscan")), ("VAc", P.mark("VAc")),
                     ("Vcar", P.mark("Vcar")), ("dSH", P.mark("dSH")),
                     ("Vb1", P.mark("Vb1")), ("Tffn", P.mark("Tffn")),
                     ("Ssct", P.mark("Ssct"))])
    emit_exchange()
    m_vhe_done[0] = P.mark("Vhe")
    emit_read(0)
    emit_read(1)
    for k in range(n_chunks):
        emit_back(k)
        emit_read(k + 2)

    if debug:
        dbg = {}
        for nm, t in [("a", a_t), ("b", b_t), ("h", h_t), ("xcb", xcb),
                      ("xntok", xn_tok), ("mvd", mv),
                      ("carryd", carry), ("acard", acar), ("hentd", hent),
                      ("gatd", gat), ("At", A_t),
                      ("g", g_t), ("xnf", xn_feat), ("wfq", wf_bf),
                      ("c4d", c4), ("qscd", qsc), ("rmstokd", rms_tok)]:
            shp = list(t.shape)
            flat = [shp[0], int(np.prod(shp[1:]))]
            dt_ = t.dtype
            od = nc.dram_tensor("dbg_" + nm, flat, dt_, kind="ExternalOutput")
            dbg[nm] = (t, od, flat)
        wl = [(nm, 16 * P.sems[nm]["c"] if P.sems[nm]["step"] == 16 else P.sems[nm]["c"])
              for nm in P.sems]
        wl = [(nm, P.sems[nm]["c"]) for nm in P.sems]
        for nm, (t, od, flat) in dbg.items():
            P.add("sync",
                  lambda e, t=t, od=od, flat=flat: e.dma_start(
                      out=od[:, :],
                      in_=bass.AP(t, 0, [[flat[1], flat[0]], [1, flat[1]]])),
                  waits=[("Vfin", P.mark("Vfin")), ("dOUT", P.mark("dOUT"))],
                  inc="dOUT")

    P.add("sync", lambda e: e.nop(),
          waits=[("dOUT", P.mark("dOUT"))])

    P.emit(nc)
    return nc


_NC_CACHE = {}


def _get_nc(n_chunks):
    if n_chunks not in _NC_CACHE:
        _NC_CACHE[n_chunks] = build(n_chunks)
    return _NC_CACHE[n_chunks]


def make_inputs(x_full, conv_w, conv_b, W_r, b_r, W_i, b_i, log_a, W_out, gamma,
                W_ffn):
    def fm(v):
        return np.ascontiguousarray(v.reshape(NF, 128).T).astype(np.float32)

    convd = np.zeros((32, 128, 128), np.float32)
    cw = np.asarray(conv_w).reshape(D, 4)
    for j in range(NF):
        for kk in range(4):
            np.fill_diagonal(convd[j * 4 + kk], cw[j * 128:(j + 1) * 128, kk])

    consts = np.zeros((128, 8), np.float32)
    consts[:, 0] = 1e-5
    consts[:, 1] = 0.5
    consts[:, 2] = 0.25
    consts[:, 3] = -0.25
    consts[:, 4] = 1e-6
    consts[:, 5] = 1.0 / D
    consts[:, 6] = -1.0

    shared = dict(
        wrT=np.ascontiguousarray(np.asarray(W_r).T),
        wiT=np.ascontiguousarray(np.asarray(W_i).T),
        woT=np.ascontiguousarray(np.asarray(W_out).T),
        wfT=np.ascontiguousarray(np.asarray(W_ffn).T),
        convd=convd,
        br_fm=fm(np.asarray(b_r)), bi_fm=fm(np.asarray(b_i)),
        cb_fm=fm(np.asarray(conv_b)), la_fm=fm(np.asarray(log_a)),
        gam_fm=fm(np.asarray(gamma)),
        consts=consts,
        eye=np.eye(128, dtype=np.float32),
        eye_bf=np.eye(128, dtype=ml_dtypes.bfloat16),
        ones_bf=np.ones((128, 1), ml_dtypes.bfloat16),
    )
    x = np.asarray(x_full)
    B, L, d = x.shape
    half = L // 2
    nonce = np.array([[np.random.randint(1, 2**31)]], np.uint32)
    in_maps = []
    for c in range(8):
        b, h = c // 2, c % 2
        m = dict(shared)
        if h == 0:
            xh = np.concatenate([np.zeros((3, d), np.float32), x[b, 0:half]], axis=0)
        else:
            xh = x[b, half - 3:L]
        m["x"] = np.ascontiguousarray(xh)
        rm = np.zeros((128, NF), np.float32)
        if h == 1:
            rm[:, :] = 1.0
        m["rmask"] = rm
        m["slot"] = np.array([[h, 1 - h, h, 0]], np.uint32)
        m["nonce"] = nonce
        in_maps.append(m)
    return in_maps


def kernel(L0_attn_x, L0_attn_conv_w, L0_attn_conv_b, L0_attn_W_r, L0_attn_b_r,
           L0_attn_W_i, L0_attn_b_i, L0_attn_log_a, L0_attn_W_out, L0_attn_gamma,
           L0_ffn_weight):
    x = np.asarray(L0_attn_x, dtype=np.float32)
    B, L, d = x.shape
    n_chunks = L // 2 // TC
    nc = _get_nc(n_chunks)
    in_maps = make_inputs(x, L0_attn_conv_w, L0_attn_conv_b, L0_attn_W_r,
                          L0_attn_b_r, L0_attn_W_i, L0_attn_b_i, L0_attn_log_a,
                          L0_attn_W_out, L0_attn_gamma, L0_ffn_weight)
    res = run_bass_kernel_spmd(nc, in_maps, core_ids=list(range(8)))
    half = L // 2
    out = np.empty((B, L, d), np.float32)
    for c in range(8):
        b, h = c // 2, c % 2
        out[b, h * half:(h + 1) * half] = res.results[c]["out"]
    return out
